# version 41
# baseline (speedup 1.0000x reference)
import numpy as np
import contextlib
import concourse.bass as bass
import concourse.mybir as mybir
from concourse.bass_utils import run_bass_kernel_spmd

F32 = mybir.dt.float32
BF16 = mybir.dt.bfloat16
ALU = mybir.AluOpType
ACT = mybir.ActivationFunctionType
AX = mybir.AxisListType

ENGINES = ("sync", "vector", "scalar", "tensor", "gpsimd")


class Ctx:

    def __init__(self, nc):
        self.nc = nc
        self.prog = {e: [] for e in ENGINES}
        self.cnt = {}
        self.sem_handles = {}
        self.last_w = {}
        self.readers = {}
        self.waited = {e: {} for e in ENGINES}

    def sem(self, name):
        if name not in self.cnt:
            self.cnt[name] = 0
        return name

    def _emit_wait(self, eng, semname, val):
        if val <= 0:
            return
        if self.waited[eng].get(semname, -1) >= val:
            return
        self.waited[eng][semname] = val
        self.prog[eng].append(("wait", semname, val))

    def op(self, eng, fns, reads=(), writes=(), sem=None, inc=1, tag=""):
        if sem is None:
            sem = "s_" + eng
        self.sem(sem)
        waits = {}
        for b in reads:
            lw = self.last_w.get(b)
            if lw:
                s, c = lw
                waits[s] = max(waits.get(s, 0), c)
        for b in writes:
            lw = self.last_w.get(b)
            if lw:
                s, c = lw
                waits[s] = max(waits.get(s, 0), c)
            for s, c in self.readers.get(b, ()):
                waits[s] = max(waits.get(s, 0), c)
        for s, c in sorted(waits.items()):
            self._emit_wait(eng, s, c)
        self.cnt[sem] += inc
        cval = self.cnt[sem]
        self.prog[eng].append(("ops", fns, sem, inc, tag))
        for b in writes:
            self.last_w[b] = (sem, cval)
            self.readers[b] = []
        for b in reads:
            self.readers.setdefault(b, []).append((sem, cval))
        return cval

    def dma(self, eng, out, in_, reads=(), writes=(), tag=""):
        sem = "qd_" + (writes[0] if writes else "out")
        self._dma_rr = getattr(self, "_dma_rr", 0) + 1
        eng = ("sync", "scalar")[self._dma_rr % 2]
        return self.op(eng, [lambda e, o=out, i=in_: e.dma_start(out=o, in_=i)],
                       reads=reads, writes=writes, sem=sem, inc=16, tag=tag)

    def phase_barrier(self):
        for e in ENGINES:
            for s in sorted(self.cnt):
                self._emit_wait(e, s, self.cnt[s])
        self.last_w = {}
        self.readers = {}

    def build(self, es):
        nc = self.nc
        for name in self.cnt:
            self.sem_handles[name] = es.enter_context(nc.semaphore(name))
        block = es.enter_context(nc.Block())
        for engname in ENGINES:
            items = self.prog[engname]
            if not items:
                continue

            def make(engname, items):
                def run(eng):
                    for it in items:
                        if it[0] == "wait":
                            _, s, v = it
                            eng.wait_ge(self.sem_handles[s], v)
                        else:
                            _, fns, s, inc, tag = it
                            inst = None
                            for fn in fns:
                                inst = fn(eng)
                            inst.then_inc(self.sem_handles[s], inc)
                return run

            getattr(block, engname)(make(engname, items))


MAGIC_CLAMP = 1.5 * 2.0 ** 14
MAGIC_SCALE = float(2.0 ** 20)


def emit_quant(ctx, W, src_key, src_ap, out_key, out_ap, wk, tag=""):
    nb = W // 32
    b3 = lambda ap: ap.rearrange("p (n b) -> p n b", b=32)
    bmt, st, rst, vt, Mt, at = wk["bm"], wk["s"], wk["rs"], wk["v"], wk["M"], wk["a"]
    kq = wk["kq"]
    kbm, ks, krs, kv, kM, ka = (kq + x for x in ("bm", "s", "rs", "v", "M", "a"))
    ctx.op("vector", [lambda e: e.reduce_max(bmt[:, 0:nb], b3(src_ap), axis=AX.X,
                                             apply_absolute_value=True)],
           reads=[src_key], writes=[kbm], tag=tag + "/bmax")
    ctx.op("vector", [lambda e: e.tensor_scalar(st[:, 0:nb], bmt[:, 0:nb], 1.0 / 448.0,
                                                1e-30, ALU.mult, ALU.max)],
           reads=[kbm], writes=[ks], tag=tag + "/s")
    ctx.op("vector", [lambda e: e.reciprocal(rst[:, 0:nb], st[:, 0:nb])],
           reads=[ks], writes=[krs], tag=tag + "/rs")
    rsb = rst[:, 0:nb].unsqueeze(2).broadcast_to([128, nb, 32])
    ctx.op("vector", [lambda e: e.tensor_tensor(b3(vt[:, 0:W]), b3(src_ap), rsb,
                                                ALU.mult)],
           reads=[krs, src_key], writes=[kv], tag=tag + "/v")
    ctx.op("scalar", [lambda e: e.activation(Mt[:, 0:W], vt[:, 0:W], ACT.Abs,
                                             scale=MAGIC_SCALE)],
           reads=[kv], writes=[kM], tag=tag + "/absM")
    ctx.op("vector", [lambda e: e.tensor_scalar_max(Mt[:, 0:W], Mt[:, 0:W],
                                                    MAGIC_CLAMP)],
           reads=[kM], writes=[kM], tag=tag + "/clamp")
    ctx.op("vector", [lambda e: e.tensor_add(at[:, 0:W], vt[:, 0:W], Mt[:, 0:W])],
           reads=[kv, kM], writes=[ka], tag=tag + "/add")
    ctx.op("vector", [lambda e: e.tensor_sub(at[:, 0:W], at[:, 0:W], Mt[:, 0:W])],
           reads=[ka, kM], writes=[ka], tag=tag + "/sub")
    sb_ = st[:, 0:nb].unsqueeze(2).broadcast_to([128, nb, 32])
    ctx.op("vector", [lambda e: e.tensor_tensor(b3(out_ap), b3(at[:, 0:W]), sb_,
                                                ALU.mult)],
           reads=[ka, ks], writes=[out_key], tag=tag + "/deq")


def emit_ln(ctx, W, src_key, src_ap, out_key, out_ap, wk, tag=""):
    inv = 1.0 / W
    xsum, xsq, junk, mean, var, rstd, m2 = (wk[x] for x in
        ("xsum", "xsq", "junk", "mean", "var", "rstd", "m2"))
    K = wk["kl"]
    jk = wk["junk_key"]
    ctx.op("vector", [lambda e: e.reduce_sum(xsum[:], src_ap, axis=AX.X)],
           reads=[src_key], writes=[K + "xsum"], tag=tag + "/sum")
    ctx.op("scalar", [lambda e: e.activation(junk[:, 0:W], src_ap, ACT.Square,
                                             accum_out=xsq[:])],
           reads=[src_key], writes=[K + "xsq", jk], tag=tag + "/sumsq")
    ctx.op("vector", [lambda e: e.tensor_scalar_mul(mean[:], xsum[:], inv)],
           reads=[K + "xsum"], writes=[K + "mean"], tag=tag + "/mean")
    ctx.op("vector", [lambda e: e.tensor_scalar_mul(var[:], xsq[:], inv)],
           reads=[K + "xsq"], writes=[K + "var"], tag=tag + "/e2")
    ctx.op("vector", [lambda e: e.tensor_mul(m2[:], mean[:], mean[:])],
           reads=[K + "mean"], writes=[K + "m2"], tag=tag + "/m2")
    ctx.op("vector", [lambda e: e.tensor_sub(var[:], var[:], m2[:])],
           reads=[K + "var", K + "m2"], writes=[K + "var"], tag=tag + "/var")
    ctx.op("vector", [lambda e: e.tensor_scalar_add(var[:], var[:], 1e-5)],
           reads=[K + "var"], writes=[K + "var"], tag=tag + "/eps")
    ctx.op("scalar", [lambda e: e.activation(rstd[:], var[:], ACT.Sqrt)],
           reads=[K + "var"], writes=[K + "rstd"], tag=tag + "/sqrt")
    ctx.op("vector", [lambda e: e.reciprocal(rstd[:], rstd[:])],
           reads=[K + "rstd"], writes=[K + "rstd"], tag=tag + "/rsqrt")
    W_ = src_ap.shape[-1]
    rstd_b = rstd[:].broadcast_to([128, W_])
    ctx.op("vector", [lambda e: e.scalar_tensor_tensor(out_ap, src_ap, mean[:],
                                                       rstd_b, ALU.subtract, ALU.mult)],
           reads=[src_key, K + "rstd", K + "mean"], writes=[out_key],
           tag=tag + "/apply")


def emit_weight_prep(ctx, t_in, t_sl, n_tiles, Wfree, n_tr, wk, sb, tag):
    wt, dqw, stg, idn = sb["wt"], sb["dqw"], sb["stg"], sb["idn"]
    for r in range(n_tiles):
        ctx.dma("sync", wt[:, 0:Wfree], t_in[r * 128:(r + 1) * 128, :],
                writes=["wt"], tag=f"{tag}/ld{r}")
        emit_quant(ctx, Wfree, "wt", wt[:, 0:Wfree], "dqw", dqw[:, 0:Wfree],
                   wk, tag=f"{tag}/q{r}")
        for j in range(n_tr):
            pbk = "pbt" + str(j % 2)
            pbt = sb[pbk]
            ctx.op("tensor",
                   [lambda e, j=j, pbt=pbt: e.transpose(pbt[:, 0:128],
                        dqw[:, j * 128:(j + 1) * 128], idn[:])],
                   reads=["dqw", "idn"], writes=[pbk], tag=f"{tag}/tr{r}_{j}")
            ctx.op("vector",
                   [lambda e, j=j, pbt=pbt: e.tensor_copy(
                       stg[:, j * 128:(j + 1) * 128], pbt[:, 0:128])],
                   reads=[pbk], writes=["stg"], tag=f"{tag}/ev{r}_{j}")
        ctx.dma("sync", t_sl[:, :, r * 128:(r + 1) * 128].transpose([1, 0, 2]),
                stg[:, 0:n_tr * 128],
                reads=["stg"], writes=[f"{tag}_sl"], tag=f"{tag}/st{r}")


def emit_ffn(ctx, sb, wk, NT, M, I, NC, gsf, usf, dsf, hscr, tag="ffn"):
    MCH = M // 128
    IC = I // 512
    ISL = I // NC
    xs, xT = sb["xs"], sb["xT"]
    xn, dqx = sb["xn"], sb["dqw"]
    gsb, hh, dqh, hTstg = sb["gsb"], sb["hh"], sb["dqh"], sb["hTstg"]

    idn = sb["idn"]
    pb = sb["pb"]
    for t in range(NT):
        xs_t = xs[:, t * M:(t + 1) * M]
        emit_ln(ctx, M, f"xs{t}", xs_t, "xn", xn[:], wk, tag=f"{tag}/ln{t}")
        emit_quant(ctx, M, "xn", xn[:], "dqw", dqx[:], wk, tag=f"{tag}/qx{t}")
        for j in range(MCH):
            pbk = "pbt" + str(j % 2)
            pbt = sb[pbk]
            ctx.op("tensor",
                   [lambda e, j=j, pbt=pbt: e.transpose(pbt[:, 0:128],
                        dqx[:, j * 128:(j + 1) * 128], idn[:])],
                   reads=["dqw", "idn"], writes=[pbk], tag=f"{tag}/xtr{t}_{j}")
            ctx.op("vector",
                   [lambda e, t=t, j=j, pbt=pbt: e.tensor_copy(
                       xT[:, t * M + j * 128: t * M + (j + 1) * 128], pbt[:, 0:128])],
                   reads=[pbk], writes=["xT"], tag=f"{tag}/xev{t}_{j}")
    NHC = I // 256
    TPC_ = NT * 128
    kqp = wk["kq"]
    ctx.op("vector", [lambda e: e.memset(wk["hbm"][:, 0:16], 0.0)],
           reads=["wt"],
           writes=["wt"] + [kqp + f"h{p}_{x}" for p in (0, 1)
                            for x in ("v", "M", "a")],
           tag=f"{tag}/hqbar")
    pending = []
    def flush_pending(ctx):
        for fn in pending:
            fn()
        pending.clear()
    for c in range(NHC):
        rank, off = (c * 256) // ISL, (c * 256) % ISL
        pw = c % 2
        wgb, wgk = (sb["wg0"], "wg0") if pw == 0 else (sb["wg1"], "wg1")
        wub, wuk = (sb["wu0"], "wu0") if pw == 0 else (sb["wu1"], "wu1")
        qg = ("sync", "scalar", "gpsimd")[c % 3]
        qu = ("scalar", "gpsimd", "sync")[c % 3]
        ctx.dma(qg, wgb[:], gsf[rank, :, :, off:off + 256].transpose([1, 0, 2]),
                reads=["wsf"], writes=[wgk], tag=f"{tag}/wg{c}")
        ctx.dma(qu, wub[:], usf[rank, :, :, off:off + 256].transpose([1, 0, 2]),
                reads=["wsf"], writes=[wuk], tag=f"{tag}/wu{c}")
        for t in range(NT):
            gb = (0, 2, 4)[t % 3]
            ub = (1, 3, 5)[t % 3]
            gate_fns = [(lambda e, t=t, j=j, wgb=wgb, gb=gb: e.matmul(
                            pb[gb][:, 0:256],
                            xT[:, t * M + j * 128: t * M + (j + 1) * 128],
                            wgb[:, j * 256:(j + 1) * 256],
                            start=(j == 0), stop=(j == MCH - 1)))
                        for j in range(MCH)]
            ctx.op("tensor", gate_fns, reads=["xT", wgk], writes=[f"pb{gb}"],
                   tag=f"{tag}/mmg{c}_{t}")
            up_fns = [(lambda e, t=t, j=j, wub=wub, ub=ub: e.matmul(
                          pb[ub][:, 0:256],
                          xT[:, t * M + j * 128: t * M + (j + 1) * 128],
                          wub[:, j * 256:(j + 1) * 256],
                          start=(j == 0), stop=(j == MCH - 1)))
                      for j in range(MCH)]
            ctx.op("tensor", up_fns, reads=["xT", wuk], writes=[f"pb{ub}"],
                   tag=f"{tag}/mmu{c}_{t}")
            flush_pending(ctx)
            p2 = t % 2
            off2 = p2 * 256
            wk2 = dict(wk)
            wk2["kq"] = wk["kq"] + f"h{p2}_"
            wk2["bm"] = wk["hbm"][:, p2 * 8:p2 * 8 + 8]
            wk2["s"] = wk["hs"][:, p2 * 8:p2 * 8 + 8]
            wk2["rs"] = wk["hrs"][:, p2 * 8:p2 * 8 + 8]
            wk2["v"] = wk["hv"][:, off2:off2 + 256]
            wk2["M"] = wk["hM"][:, off2:off2 + 256]
            wk2["a"] = wk["ha"][:, off2:off2 + 256]
            ctx.op("scalar", [lambda e, gb=gb, off2=off2: e.activation(
                       gsb[:, off2:off2 + 256], pb[gb][:, 0:256], ACT.Silu)],
                   reads=[f"pb{gb}"], writes=[f"gsb{p2}"],
                   tag=f"{tag}/silu{c}_{t}")
            ctx.op("vector", [lambda e, ub=ub, off2=off2: e.tensor_mul(
                       hh[:, off2:off2 + 256], gsb[:, off2:off2 + 256],
                       pb[ub][:, 0:256])],
                   reads=[f"gsb{p2}", f"pb{ub}"], writes=[f"hh{p2}"],
                   tag=f"{tag}/hmul{c}_{t}")
            emit_quant(ctx, 256, f"hh{p2}", hh[:, off2:off2 + 256],
                       f"dqh{p2}", dqh[:, off2:off2 + 256], wk2,
                       tag=f"{tag}/qh{c}_{t}")

            def make_tail(c=c, t=t, p2=p2, off2=off2):
                def tail():
                    for s in range(2):
                        pbk = "pbt" + str(s % 2)
                        pbt = sb[pbk]
                        ctx.op("tensor",
                               [lambda e, s=s, pbt=pbt, off2=off2: e.transpose(
                                    pbt[:, 0:128],
                                    dqh[:, off2 + s * 128:off2 + (s + 1) * 128],
                                    idn[:])],
                               reads=[f"dqh{p2}", "idn"], writes=[pbk],
                               tag=f"{tag}/htr{c}_{t}_{s}")
                        ctx.op("vector",
                               [lambda e, s=s, t=t, pbt=pbt: e.tensor_copy(
                                   hTstg[:, s * TPC_ + t * 128:
                                         s * TPC_ + (t + 1) * 128], pbt[:, 0:128])],
                               reads=[pbk], writes=["hTstg"],
                               tag=f"{tag}/hev{c}_{t}_{s}")
                    if t == NT - 1:
                        for s in range(2):
                            ctx.dma("sync", hscr[c * 2 + s, :, :],
                                    hTstg[:, s * TPC_:(s + 1) * TPC_],
                                    reads=["hTstg"], writes=["hscr"],
                                    tag=f"{tag}/hst{c}_{s}")
                return tail
            pending.append(make_tail())
    flush_pending(ctx)
    KCH = I // 128
    HPW = M // 4
    for hp in range(4):
        for kc in range(KCH):
            rank, idx = (kc * 128) // ISL, ((kc * 128) % ISL) // 128
            wdb = sb[f"wd{kc % 4}"]
            wdk = f"wd{kc % 4}"
            htb = sb[f"ht{kc % 4}"]
            htk = f"ht{kc % 4}"
            ctx.dma(("sync", "scalar", "gpsimd")[kc % 3], wdb[:, 0:HPW],
                    dsf[rank, idx, :, hp * HPW:(hp + 1) * HPW],
                    reads=["wsf"], writes=[wdk], tag=f"{tag}/wd{hp}_{kc}")
            ctx.dma(("gpsimd", "sync", "scalar")[kc % 3], htb[:], hscr[kc, :, :],
                    reads=["hscr"], writes=[htk], tag=f"{tag}/htl{hp}_{kc}")
            mm_fns = [(lambda e, t=t, kc=kc, htb=htb, wdb=wdb: e.matmul(
                          pb[t][:, 0:HPW], htb[:, t * 128:(t + 1) * 128],
                          wdb[:, 0:HPW],
                          start=(kc == 0), stop=(kc == KCH - 1)))
                      for t in range(NT)]
            ctx.op("tensor", mm_fns, reads=[wdk, htk],
                   writes=[f"pbB{t}" for t in range(NT)], tag=f"{tag}/mmd{hp}_{kc}")
        for t in range(NT):
            xs_sl = xs[:, t * M + hp * HPW: t * M + (hp + 1) * HPW]
            ctx.op("vector",
                   [lambda e, t=t, xs_sl=xs_sl: e.scalar_tensor_tensor(
                       xs_sl, pb[t][:, 0:HPW], 0.5, xs_sl, ALU.mult, ALU.add)],
                   reads=[f"pbB{t}", f"xs{t}"], writes=[f"xs{t}"],
                   tag=f"{tag}/acc{hp}_{t}")


def emit_rope_evac(ctx, psum, psum_key, out_ap, out_key, N, cosb, sinb, tmp1, tmp2,
                   tag):
    pe_ap, po_ap = psum[0:64, 0:N], psum[64:128, 0:N]
    ctx.op("vector", [lambda e: e.tensor_mul(tmp1[0:64, 0:N], pe_ap, cosb)],
           reads=[psum_key, "ropetbl"], writes=["rtmp1"], tag=tag + "/ec")
    ctx.op("vector", [lambda e: e.tensor_mul(tmp2[0:64, 0:N], po_ap, sinb)],
           reads=[psum_key, "ropetbl"], writes=["rtmp2"], tag=tag + "/os")
    ctx.op("vector", [lambda e: e.tensor_sub(out_ap[0:64, 0:N], tmp1[0:64, 0:N],
                                             tmp2[0:64, 0:N])],
           reads=["rtmp1", "rtmp2"], writes=[out_key], tag=tag + "/re")
    ctx.op("vector", [lambda e: e.tensor_mul(tmp1[0:64, 0:N], pe_ap, sinb)],
           reads=[psum_key, "ropetbl"], writes=["rtmp1"], tag=tag + "/es")
    ctx.op("vector", [lambda e: e.tensor_mul(tmp2[0:64, 0:N], po_ap, cosb)],
           reads=[psum_key, "ropetbl"], writes=["rtmp2"], tag=tag + "/oc")
    ctx.op("vector", [lambda e: e.tensor_add(out_ap[64:128, 0:N], tmp1[0:64, 0:N],
                                             tmp2[0:64, 0:N])],
           reads=["rtmp1", "rtmp2"], writes=[out_key], tag=tag + "/ro")


def build_fused_program(NC=8, TPC=1024, M=2048, I=8192, D=128):
    T = NC * TPC
    NT = TPC // 128
    NTG = T // 128
    MCH = M // 128
    ISL = I // NC
    nc = bass.Bass("TRN2", target_bir_lowering=False)
    x_in = nc.dram_tensor("x", [TPC, M], F32, kind="ExternalInput").ap()
    gw = nc.dram_tensor("gw", [ISL, M], F32, kind="ExternalInput").ap()
    uw = nc.dram_tensor("uw", [ISL, M], F32, kind="ExternalInput").ap()
    dw = nc.dram_tensor("dw", [M, ISL], F32, kind="ExternalInput").ap()
    wak = nc.dram_tensor("wak", [M, D], F32, kind="ExternalInput").ap()
    wav = nc.dram_tensor("wav", [M, D], F32, kind="ExternalInput").ap()
    waq0 = nc.dram_tensor("waq0", [M, D], F32, kind="ExternalInput").ap()
    waq1 = nc.dram_tensor("waq1", [M, D], F32, kind="ExternalInput").ap()
    wao0 = nc.dram_tensor("wao0", [D, M], F32, kind="ExternalInput").ap()
    wao1 = nc.dram_tensor("wao1", [D, M], F32, kind="ExternalInput").ap()
    idn_in = nc.dram_tensor("idn", [128, 128], F32, kind="ExternalInput").ap()
    idnb_in = nc.dram_tensor("idnb", [128, 128], BF16, kind="ExternalInput").ap()
    dmask_in = nc.dram_tensor("dmask", [128, 128], F32, kind="ExternalInput").ap()
    ones_in = nc.dram_tensor("onesv", [128, 1], BF16, kind="ExternalInput").ap()
    zmask_in = nc.dram_tensor("zmask", [128, 128], BF16, kind="ExternalInput").ap()
    cos_in = nc.dram_tensor("cost", [64, T], F32, kind="ExternalInput").ap()
    sin_in = nc.dram_tensor("sint", [64, T], F32, kind="ExternalInput").ap()
    xout = nc.dram_tensor("xout", [TPC, M], F32, kind="ExternalOutput").ap()
    gsl = nc.dram_tensor("gsl", [MCH, 128, ISL], BF16)
    usl = nc.dram_tensor("usl", [MCH, 128, ISL], BF16)
    dsl = nc.dram_tensor("dsl", [ISL // 128, 128, M], BF16)
    gsf_t = nc.dram_tensor("gsf", [NC, MCH, 128, ISL], BF16)
    usf_t = nc.dram_tensor("usf", [NC, MCH, 128, ISL], BF16)
    dsf_t = nc.dram_tensor("dsf", [NC, ISL // 128, 128, M], BF16)
    hscr_t = nc.dram_tensor("hscr", [I // 128, 128, TPC], BF16)
    xnsl = nc.dram_tensor("xnsl", [MCH, 128, TPC], BF16)
    xnsf = nc.dram_tensor("xnsf", [NC, MCH, 128, TPC], BF16)
    rscr = nc.dram_tensor("rscr", [T, M], BF16)
    rsum = nc.dram_tensor("rsum", [TPC, M], BF16)
    gsf, usf, dsf, hscr = gsf_t.ap(), usf_t.ap(), dsf_t.ap(), hscr_t.ap()

    ctx = Ctx(nc)
    with contextlib.ExitStack() as eg:
        E = eg.enter_context
        xs = E(nc.sbuf_tensor("xs", [128, NT * M], F32))
        idn = E(nc.sbuf_tensor("idn_sb", [128, 128], F32))
        idnb = E(nc.sbuf_tensor("idnb_sb", [128, 128], BF16))
        dmask = E(nc.sbuf_tensor("dmask_sb", [128, 128], F32))
        lnw = {"kl": "ln_", "kq": "q_"}
        for n in ("xsum", "xsq", "mean", "var", "rstd", "m2"):
            lnw[n] = E(nc.sbuf_tensor("ln_" + n, [128, 1], F32))
        nm = [E(nc.sbuf_tensor(f"nm{h}", [128, 1], F32)) for h in range(2)]
        den = E(nc.sbuf_tensor("den", [128, 1], F32))
        rden = [E(nc.sbuf_tensor(f"rden{h}", [128, 1], F32)) for h in range(2)]
        mx = E(nc.sbuf_tensor("mx", [128, 64], F32))
        OTb = E(nc.sbuf_tensor("OTb", [128, 256], BF16))

        ctx.op("vector", [lambda e: e.memset(mx[:], 0.0)], writes=["mx"],
               tag="mxinit")
        ctx.dma("sync", idn[:], idn_in, writes=["idn"], tag="idn")
        ctx.dma("sync", idnb[:], idnb_in, writes=["idnb"], tag="idnb")
        ctx.dma("sync", dmask[:], dmask_in, writes=["dmask"], tag="dmask")

        def alloc_ffn(es, sfx):
            sb = {"xs": xs, "idn": idn}
            EE = es.enter_context
            sb["xT"] = EE(nc.sbuf_tensor("xT" + sfx, [128, NT * M], BF16))
            sb["wt"] = EE(nc.sbuf_tensor("wt" + sfx, [128, M], F32))
            sb["dqw"] = EE(nc.sbuf_tensor("dqw" + sfx, [128, M], F32))
            sb["stg"] = EE(nc.sbuf_tensor("stg" + sfx, [128, MCH * 128], BF16))
            sb["xn"] = EE(nc.sbuf_tensor("xn" + sfx, [128, M], F32))
            sb["wg0"] = EE(nc.sbuf_tensor("wg0" + sfx, [128, MCH * 256], BF16))
            sb["wg1"] = EE(nc.sbuf_tensor("wg1" + sfx, [128, MCH * 256], BF16))
            sb["wu0"] = EE(nc.sbuf_tensor("wu0" + sfx, [128, MCH * 256], BF16))
            sb["wu1"] = EE(nc.sbuf_tensor("wu1" + sfx, [128, MCH * 256], BF16))
            sb["gsb"] = EE(nc.sbuf_tensor("gsb" + sfx, [128, 512], F32))
            sb["hh"] = EE(nc.sbuf_tensor("hh" + sfx, [128, 512], F32))
            sb["dqh"] = EE(nc.sbuf_tensor("dqh" + sfx, [128, 512], F32))
            sb["hTstg"] = EE(nc.sbuf_tensor("hTstg" + sfx, [128, 2 * NT * 128], BF16))
            for i4 in range(4):
                sb[f"wd{i4}"] = EE(nc.sbuf_tensor(f"wd{i4}" + sfx, [128, 512], BF16))
                sb[f"ht{i4}"] = EE(nc.sbuf_tensor(f"ht{i4}" + sfx, [128, TPC], BF16))
            wk = dict(lnw)
            wk["bm"] = EE(nc.sbuf_tensor("q_bm" + sfx, [128, M // 32], F32))
            wk["s"] = EE(nc.sbuf_tensor("q_s" + sfx, [128, M // 32], F32))
            wk["rs"] = EE(nc.sbuf_tensor("q_rs" + sfx, [128, M // 32], F32))
            wk["v"] = EE(nc.sbuf_tensor("q_v" + sfx, [128, M], F32))
            wk["M"] = EE(nc.sbuf_tensor("q_M" + sfx, [128, M], F32))
            wk["a"] = EE(nc.sbuf_tensor("q_a" + sfx, [128, M], F32))
            wk["junk"] = wk["v"]
            wk["junk_key"] = "q_v"
            wk["hbm"] = EE(nc.sbuf_tensor("qh_bm" + sfx, [128, 16], F32))
            wk["hs"] = EE(nc.sbuf_tensor("qh_s" + sfx, [128, 16], F32))
            wk["hrs"] = EE(nc.sbuf_tensor("qh_rs" + sfx, [128, 16], F32))
            if M >= 1024:
                wk["hv"] = sb["wt"][:, 0:512]
                wk["hM"] = sb["wt"][:, 512:1024]
            else:
                wk["hv"] = EE(nc.sbuf_tensor("qh_v" + sfx, [128, 512], F32))
                wk["hM"] = EE(nc.sbuf_tensor("qh_M" + sfx, [128, 512], F32))
            wk["ha"] = wk["hv"]
            pb = [EE(nc.psum_tensor(f"pb{i}" + sfx, [128, 512], F32))
                  for i in range(8)]
            sb["pb"] = pb
            sb["pbt0"] = pb[6]
            sb["pbt1"] = pb[7]
            return sb, wk

        with contextlib.ExitStack() as e1:
            sb, wk = alloc_ffn(e1, "_f1")
            for t in range(NT):
                ctx.dma("gpsimd", xs[:, t * M:(t + 1) * M],
                        x_in[t * 128:(t + 1) * 128, :], writes=[f"xs{t}"],
                        tag=f"ldx{t}")
            for t_in, t_sl, t_sf, nt_, wf_, ntr_, k in (
                    (gw, gsl, gsf_t, ISL // 128, M, MCH, "g"),
                    (uw, usl, usf_t, ISL // 128, M, MCH, "u"),
                    (dw, dsl, dsf_t, MCH, ISL, ISL // 128, "d")):
                emit_weight_prep(ctx, t_in, t_sl.ap(), nt_, wf_, ntr_, wk, sb, k)
                ctx.op("gpsimd",
                       [lambda e, t_sl=t_sl, t_sf=t_sf: e.collective_compute(
                           "AllGather", ALU.bypass,
                           replica_groups=[list(range(NC))],
                           ins=[t_sl.ap().opt()], outs=[t_sf.ap().opt()])],
                       reads=[f"{k}_sl"], writes=["wsf"], sem="s_cc",
                       tag=f"ag_{k}")
            emit_ffn(ctx, sb, wk, NT, M, I, NC, gsf, usf, dsf, hscr, tag="f1")
        ctx.phase_barrier()

        with contextlib.ExitStack() as e2:
            EE = e2.enter_context
            asb = {"xs": xs, "idn": idn}
            asb["xn"] = EE(nc.sbuf_tensor("xn_at", [128, M], F32))
            asb["stg"] = EE(nc.sbuf_tensor("stg_at", [128, MCH * 128], BF16))
            wakb = EE(nc.sbuf_tensor("wakb", [128, MCH * 128], BF16))
            wavb = EE(nc.sbuf_tensor("wavb", [128, MCH * 128], BF16))
            waqb = [EE(nc.sbuf_tensor(f"waqb{h}", [128, MCH * 128], BF16))
                    for h in range(2)]
            waob = [EE(nc.sbuf_tensor(f"waob{h}", [128, M], BF16)) for h in range(2)]
            KT = EE(nc.sbuf_tensor("KT", [128, T], BF16))
            Vb = EE(nc.sbuf_tensor("Vb", [128, T], BF16))
            xch = EE(nc.sbuf_tensor("xch", [128, MCH * 512], BF16))
            xqch = xch
            QTt = [EE(nc.sbuf_tensor(f"QTt{h}", [128, 256], BF16)) for h in range(2)]
            cosb = EE(nc.sbuf_tensor("cosb", [64, 512], F32))
            sinb = EE(nc.sbuf_tensor("sinb", [64, 512], F32))
            rtmp1 = EE(nc.sbuf_tensor("rtmp1", [64, 512], F32))
            rtmp2 = EE(nc.sbuf_tensor("rtmp2", [64, 512], F32))
            Pb = EE(nc.sbuf_tensor("Pb", [128, 6 * 512], BF16))
            q2b = EE(nc.sbuf_tensor("q2b", [128, 256], BF16))
            mqs = EE(nc.sbuf_tensor("mqs", [128, 4], F32))
            nmb = EE(nc.sbuf_tensor("nmb", [128, 4], F32))
            maxk2b = EE(nc.sbuf_tensor("maxk2b", [128, 1], F32))
            mxk = EE(nc.sbuf_tensor("mxk", [128, 17], F32))
            sc1b = EE(nc.sbuf_tensor("sc1b", [1, 1], BF16))
            ones_sb = EE(nc.sbuf_tensor("ones_sb", [128, 1], BF16))
            zmask = EE(nc.sbuf_tensor("zmask_sb", [128, 128], BF16))
            PTstgH = [EE(nc.sbuf_tensor(f"PTstg{h}", [128, 1024], BF16))
                      for h in range(2)]
            rstg = EE(nc.sbuf_tensor("rstg", [128, M], BF16))
            rstgf = asb["xn"]
            rsb = rstg
            awk = dict(lnw)
            awk["junk"] = asb["xn"]
            awk["junk_key"] = "xn"
            pb = [EE(nc.psum_tensor(f"apb{i}", [128, 512], F32))
                  for i in (0, 2, 3, 4, 5, 6)]
            pb = {i: t for i, t in zip((0, 2, 3, 4, 5, 6), pb)}
            ptpsH = [EE(nc.psum_tensor(f"ptps{h}", [128, 1024], BF16))
                     for h in range(2)]
            asb["pbt0"] = pb[5]
            asb["pbt1"] = pb[6]

            wtmp = asb["xn"]
            for src, dstb, k in ((wak, wakb, "wakb"), (wav, wavb, "wavb"),
                                 (waq0, waqb[0], "waqb0"), (waq1, waqb[1], "waqb1")):
                ctx.dma("sync", wtmp[:], src.rearrange("(j p) d -> p j d", p=128),
                        writes=["xn"], tag=f"ldw_{k}")
                ctx.op("vector", [lambda e, dstb=dstb: e.tensor_copy(dstb[:],
                                                                     wtmp[:])],
                       reads=["xn"], writes=[k], tag=f"cast_{k}")
            for h in range(2):
                src = wao0 if h == 0 else wao1
                ctx.dma("sync", wtmp[:, 0:M], src, writes=["xn"], tag=f"ldwao{h}")
                ctx.op("vector", [lambda e, h=h: e.tensor_copy(waob[h][:],
                                                               wtmp[:, 0:M])],
                       reads=["xn"], writes=[f"waob{h}"], tag=f"cast_wao{h}")

            ctx.dma("sync", ones_sb[:], ones_in, writes=["ones"], tag="ldones")
            ctx.dma("sync", zmask[:], zmask_in, writes=["zmask"], tag="ldzmask")
            for t in range(NT):
                xs_t = xs[:, t * M:(t + 1) * M]
                emit_ln(ctx, M, f"xs{t}", xs_t, "xn", asb["xn"][:], awk,
                        tag=f"lnT/ln{t}")
                for j in range(MCH):
                    pbk = "pbt" + str(j % 2)
                    pbt = asb[pbk]
                    ctx.op("tensor",
                           [lambda e, j=j, pbt=pbt: e.transpose(pbt[:, 0:128],
                                asb["xn"][:, j * 128:(j + 1) * 128], idn[:])],
                           reads=["xn", "idn"], writes=[pbk], tag=f"lnT/tr{t}_{j}")
                    ctx.op("vector",
                           [lambda e, j=j, pbt=pbt: e.tensor_copy(
                               asb["stg"][:, j * 128:(j + 1) * 128], pbt[:, 0:128])],
                           reads=[pbk], writes=["stg"], tag=f"lnT/ev{t}_{j}")
                ctx.dma("sync", xnsl.ap()[:, :, t * 128:(t + 1) * 128].transpose([1, 0, 2]),
                        asb["stg"][:, 0:MCH * 128],
                        reads=["stg"], writes=["xnsl"], tag=f"lnT/st{t}")
            ctx.op("gpsimd",
                   [lambda e: e.collective_compute(
                       "AllGather", ALU.bypass, replica_groups=[list(range(NC))],
                       ins=[xnsl.ap().opt()], outs=[xnsf.ap().opt()])],
                   reads=["xnsl"], writes=["xnsf"], sem="s_cc", tag="ag_xn")

            xnsf_ap = xnsf.ap()
            for tc in range(T // 512):
                rank, off = tc // (TPC // 512), (tc % (TPC // 512)) * 512
                ctx.dma("sync", xch[:],
                        xnsf_ap[rank, :, :, off:off + 512].transpose([1, 0, 2]),
                        reads=["xnsf"], writes=["xch"], tag=f"xch{tc}")
                ctx.dma("gpsimd", cosb[:], cos_in[:, tc * 512:(tc + 1) * 512],
                        writes=["ropetbl"], tag=f"cos{tc}")
                ctx.dma("gpsimd", sinb[:], sin_in[:, tc * 512:(tc + 1) * 512],
                        writes=["ropetbl"], tag=f"sin{tc}")
                fns = [(lambda e, j=j: e.matmul(pb[0][:],
                            wakb[:, j * 128:(j + 1) * 128],
                            xch[:, j * 512:(j + 1) * 512],
                            start=(j == 0), stop=(j == MCH - 1)))
                       for j in range(MCH)]
                ctx.op("tensor", fns, reads=["wakb", "xch"], writes=["apb0"],
                       tag=f"mmK{tc}")
                emit_rope_evac(ctx, pb[0], "apb0", KT[:, tc * 512:(tc + 1) * 512],
                               "KT", 512, cosb[:, 0:512], sinb[:, 0:512],
                               rtmp1, rtmp2, f"ropeK{tc}")
                for s in range(4):
                    fns = [(lambda e, j=j, s=s: e.matmul(
                                pb[3][:, s * 128:(s + 1) * 128],
                                xch[:, j * 512 + s * 128: j * 512 + (s + 1) * 128],
                                wavb[:, j * 128:(j + 1) * 128],
                                start=(j == 0), stop=(j == MCH - 1)))
                           for j in range(MCH)]
                    ctx.op("tensor", fns, reads=["wavb", "xch"], writes=["apb3"],
                           tag=f"mmV{tc}_{s}")
                ctx.op("vector", [lambda e, tc=tc: e.tensor_copy(
                           Vb[:, tc * 512:(tc + 1) * 512], pb[3][:])],
                       reads=["apb3"], writes=["Vb"], tag=f"evV{tc}")

            rbanks = [0, 1, 5, 6]
            ksq = Pb
            ctx.op("vector", [lambda e: e.memset(mxk[:], 0.0)], writes=["mxk"],
                   tag="mxkinit")
            for tc in range(T // 512):
                ctx.op("vector", [lambda e, tc=tc: e.tensor_mul(
                           ksq[:, 0:512], KT[:, tc * 512:(tc + 1) * 512],
                           KT[:, tc * 512:(tc + 1) * 512])],
                       reads=["KT"], writes=["Pb0"], tag=f"ksq{tc}")
                ctx.op("tensor", [lambda e: e.matmul(pb[3][0:1, 0:512], ones_sb[:],
                                                     ksq[:, 0:512], start=True,
                                                     stop=True)],
                       reads=["Pb0", "ones"], writes=["apb3"], tag=f"kn{tc}")
                ctx.op("vector", [lambda e, tc=tc: e.reduce_max(
                           mxk[0:1, tc:tc + 1], pb[3][0:1, 0:512], axis=AX.X)],
                       reads=["apb3"], writes=["mxk"], tag=f"knm{tc}")
            ctx.op("vector", [lambda e: e.reduce_max(mxk[0:1, 16:17],
                                                     mxk[0:1, 0:16], axis=AX.X)],
                   reads=["mxk"], writes=["mxk"], tag="knmax")
            ctx.op("vector", [lambda e: e.tensor_copy(sc1b[:], mxk[0:1, 16:17])],
                   reads=["mxk"], writes=["sc1b"], tag="kncast")
            ctx.op("tensor", [lambda e: e.matmul(pb[3][:, 0:1],
                       ones_sb[0:1, 0:1].broadcast_to([1, 128]), sc1b[:],
                       start=True, stop=True)],
                   reads=["sc1b", "ones"], writes=["apb3"], tag="knbc")
            ctx.op("vector", [lambda e: e.tensor_copy(maxk2b[:], pb[3][:, 0:1])],
                   reads=["apb3"], writes=["maxk2b"], tag="knbc2")

            class _QCollect:
                def __init__(s):
                    s.q = []

                def op(s, *a, **kw):
                    s.q.append((a, kw))

                def pop1(s):
                    if s.q:
                        a, kw = s.q.pop(0)
                        ctx.op(*a, **kw)

                def flush(s):
                    while s.q:
                        s.pop1()

            qctx = _QCollect()

            def emit_qproj(jt):
                par = jt % 2
                rank, off = jt // NT, (jt % NT) * 128
                QW = MCH * 128
                ctx.dma("sync", xqch[:, par * QW:(par + 1) * QW],
                        xnsf_ap[rank, :, :, off:off + 128].transpose([1, 0, 2]),
                        reads=["xnsf"], writes=[f"xq{par}", "xch"], tag=f"xq{jt}")
                ctx.dma("gpsimd", cosb[:, par * 128:(par + 1) * 128],
                        cos_in[:, jt * 128:(jt + 1) * 128],
                        writes=[f"rt{par}", "ropetbl"], tag=f"cosq{jt}")
                ctx.dma("gpsimd", sinb[:, par * 128:(par + 1) * 128],
                        sin_in[:, jt * 128:(jt + 1) * 128],
                        writes=[f"rt{par}", "ropetbl"], tag=f"sinq{jt}")
                for h in range(2):
                    fns = [(lambda e, j=j, h=h, par=par: e.matmul(
                                pb[5][:, h * 128:(h + 1) * 128],
                                waqb[h][:, j * 128:(j + 1) * 128],
                                xqch[:, par * QW + j * 128: par * QW + (j + 1) * 128],
                                start=(j == 0), stop=(j == MCH - 1)))
                           for j in range(MCH)]
                    qctx.op("tensor", fns, reads=[f"waqb{h}", f"xq{par}"],
                           writes=["apb5q"], tag=f"mmQ{jt}_{h}")
                    pe_ap = pb[5][0:64, h * 128:(h + 1) * 128]
                    po_ap = pb[5][64:128, h * 128:(h + 1) * 128]
                    csl = cosb[:, par * 128:(par + 1) * 128]
                    ssl = sinb[:, par * 128:(par + 1) * 128]
                    out_ap = QTt[h][:, par * 128:(par + 1) * 128]
                    ok = f"QTt{h}_{par}"
                    qctx.op("vector", [lambda e, a=pe_ap, c=csl: e.tensor_mul(
                               rtmp1[0:64, 0:128], a, c)],
                           reads=["apb5q", f"rt{par}"], writes=["rtmp1"],
                           tag=f"rq{jt}_{h}/ec")
                    qctx.op("vector", [lambda e, a=po_ap, s=ssl: e.tensor_mul(
                               rtmp2[0:64, 0:128], a, s)],
                           reads=["apb5q", f"rt{par}"], writes=["rtmp2"],
                           tag=f"rq{jt}_{h}/os")
                    qctx.op("vector", [lambda e, o=out_ap: e.tensor_sub(
                               o[0:64, 0:128], rtmp1[0:64, 0:128], rtmp2[0:64, 0:128])],
                           reads=["rtmp1", "rtmp2"], writes=[ok], tag=f"rq{jt}_{h}/re")
                    qctx.op("vector", [lambda e, a=pe_ap, s=ssl: e.tensor_mul(
                               rtmp1[0:64, 0:128], a, s)],
                           reads=["apb5q", f"rt{par}"], writes=["rtmp1"],
                           tag=f"rq{jt}_{h}/es")
                    qctx.op("vector", [lambda e, a=po_ap, c=csl: e.tensor_mul(
                               rtmp2[0:64, 0:128], a, c)],
                           reads=["apb5q", f"rt{par}"], writes=["rtmp2"],
                           tag=f"rq{jt}_{h}/oc")
                    qctx.op("vector", [lambda e, o=out_ap: e.tensor_add(
                               o[64:128, 0:128], rtmp1[0:64, 0:128],
                               rtmp2[0:64, 0:128])],
                           reads=["rtmp1", "rtmp2"], writes=[ok], tag=f"rq{jt}_{h}/ro")
                    qctx.op("vector", [lambda e, o=out_ap, h=h, par=par: e.tensor_mul(
                               q2b[:, h * 128:(h + 1) * 128], o, o)],
                           reads=[ok], writes=["q2b"], tag=f"q2_{jt}_{h}")
                    qctx.op("tensor", [lambda e, h=h: e.matmul(
                               pb[5][:, 256 + h:257 + h],
                               q2b[:, h * 128:(h + 1) * 128], ones_sb[:],
                               start=True, stop=True)],
                           reads=["q2b", "ones", "apb5q"], writes=["apb5q"],
                           tag=f"q2m{jt}_{h}")
                    qctx.op("scalar", [lambda e, h=h, par=par: e.activation(
                               mqs[:, (2 * h + par):(2 * h + par) + 1],
                               pb[5][:, 256 + h:257 + h], ACT.Sqrt,
                               scale=maxk2b[:])],
                           reads=["apb5q", "maxk2b"], writes=["mqs"],
                           tag=f"mq{jt}_{h}")
                    qctx.op("vector", [lambda e, h=h, par=par: e.tensor_scalar_mul(
                               nmb[:, (2 * h + par):(2 * h + par) + 1],
                               mqs[:, (2 * h + par):(2 * h + par) + 1], -1.0)],
                           reads=["mqs"], writes=[f"nmb{h}_{par}", "qm_done"],
                           tag=f"nmq{jt}_{h}")

            emit_qproj(0)
            qctx.flush()
            pending = []

            def drain(n=2):
                for _ in range(min(n, len(pending))):
                    pending.pop(0)()

            def drain_all():
                while pending:
                    pending.pop(0)()

            for jt in range(NTG):
                par = jt % 2
                n512 = (jt * 128) // 512
                nrem = jt - n512 * 4
                kext = (jt + 1) * 128
                nblk = jt + 1
                qctx.flush()
                if jt + 1 < NTG:
                    emit_qproj(jt + 1)
                drain(2)
                chunks = [(c * 512, 512, False) for c in range(n512)]
                if nrem:
                    chunks.append((n512 * 512, nrem * 128, False))
                chunks.append((jt * 128, 128, True))
                seq = []
                for ci in range(len(chunks)):
                    seq.append((0, ci))
                    seq.append((1, ci))
                sbank = [3, 4]
                obank = [2, 6]
                otk = ["apb2", "apb6"]

                def stage_a(h, ci):
                    koff, kw, is_diag = chunks[ci]
                    bk = sbank[h]
                    qsl = QTt[h][:, par * 128:(par + 1) * 128]
                    qk = f"QTt{h}_{par}"
                    nmap = nmb[:, (2 * h + par):(2 * h + par) + 1]
                    nmk = f"nmb{h}_{par}"
                    sl = (3 * h + (ci % 3)) * 512
                    ctx.op("tensor", [lambda e, koff=koff, kw=kw, qsl=qsl, bk=bk:
                               e.matmul(pb[bk][:, 0:kw], qsl,
                                        KT[:, koff:koff + kw],
                                        start=True, stop=True)],
                           reads=[qk, "KT"], writes=[f"apb{bk}"],
                           tag=f"mmS{jt}_{h}_{ci}")
                    if not is_diag:
                        ctx.op("scalar", [lambda e, kw=kw, bk=bk, sl=sl, ci=ci,
                                          h=h, nmap=nmap: e.activation(
                                   Pb[:, sl:sl + kw], pb[bk][:, 0:kw], ACT.Exp,
                                   bias=nmap,
                                   accum_out=mx[:, h * 32 + ci:h * 32 + ci + 1])],
                               reads=[f"apb{bk}", nmk, "mx"],
                               writes=[f"Pb{sl}", "mx"], tag=f"exp{jt}_{h}_{ci}")
                    else:
                        ctx.op("scalar", [lambda e, kw=kw, bk=bk, sl=sl,
                                          nmap=nmap: e.activation(
                                   Pb[:, sl:sl + kw], pb[bk][:, 0:kw], ACT.Exp,
                                   bias=nmap)],
                               reads=[f"apb{bk}", nmk], writes=[f"Pb{sl}"],
                               tag=f"expd{jt}_{h}")
                        ctx.op("vector", [lambda e, sl=sl: e.tensor_mul(
                                   Pb[:, sl:sl + 128], Pb[:, sl:sl + 128],
                                   zmask[:])],
                               reads=[f"Pb{sl}", "zmask"], writes=[f"Pb{sl}"],
                               tag=f"pmask{jt}_{h}")
                        ctx.op("vector", [lambda e, sl=sl, ci=ci, h=h:
                                   e.reduce_sum(
                                       mx[:, h * 32 + ci:h * 32 + ci + 1],
                                       Pb[:, sl:sl + 128], axis=AX.X)],
                               reads=[f"Pb{sl}"], writes=["mx"],
                               tag=f"psum{jt}_{h}")

                def stage_b1(h, ci):
                    koff, kw, is_diag = chunks[ci]
                    sl = (3 * h + (ci % 3)) * 512
                    base_blk = koff // 128
                    nb = kw // 128
                    slot = (ci % 2) * 512
                    pt = ptpsH[h]
                    ptk = f"ptps{h}_{ci % 2}"
                    pstg = PTstgH[h]
                    pstk = f"PTstg{h}_{ci % 2}"
                    tr_fns = [(lambda e, u=u, sl=sl, pt=pt, slot=slot:
                               e.matmul(
                                   pt[:, slot + u * 128:slot + (u + 1) * 128],
                                   Pb[:, sl + u * 128:sl + (u + 1) * 128],
                                   idnb[:],
                                   is_transpose=True, skip_group_check=True))
                              for u in range(nb)]
                    ctx.op("tensor", tr_fns, reads=[f"Pb{sl}", "idnb"],
                           writes=[ptk], tag=f"trP{jt}_{h}_{base_blk}")
                    ctx.op("vector", [lambda e, nb=nb, pstg=pstg, pt=pt,
                                      slot=slot:
                               e.tensor_copy(pstg[:, slot:slot + nb * 128],
                                             pt[:, slot:slot + nb * 128])],
                           reads=[ptk], writes=[pstk],
                           tag=f"evP{jt}_{h}_{base_blk}")

                def stage_b2(h, ci):
                    koff, kw, is_diag = chunks[ci]
                    base_blk = koff // 128
                    nb = kw // 128
                    slot = (ci % 2) * 512
                    pstg = PTstgH[h]
                    pstk = f"PTstg{h}_{ci % 2}"
                    pv_fns = [(lambda e, b=base_blk + u, u=u, nblk=nblk,
                                      h=h, pstg=pstg, slot=slot: e.matmul(
                                   pb[obank[h]][:, 0:128],
                                   Vb[:, b * 128:(b + 1) * 128],
                                   pstg[:, slot + u * 128:slot + (u + 1) * 128],
                                   start=(b == 0), stop=(b == nblk - 1)))
                              for u in range(nb)]
                    pv_reads = ["Vb", pstk]
                    if base_blk == 0:
                        pv_reads.append("qm_done")
                    ctx.op("tensor", pv_fns, reads=pv_reads,
                           writes=[otk[h]], tag=f"pv{jt}_{h}_{base_blk}")

                DEPTH_T = 4
                DEPTH_V = 5
                NSEQ = len(seq)
                for k in range(NSEQ + DEPTH_V):
                    if k < NSEQ:
                        stage_a(*seq[k])
                    if 0 <= k - DEPTH_T < NSEQ:
                        stage_b1(*seq[k - DEPTH_T])
                    if 0 <= k - DEPTH_V < NSEQ:
                        stage_b2(*seq[k - DEPTH_V])
                    if k >= 1:
                        drain(1)
                    qctx.pop1()
                drain_all()
                nchs = len(chunks)
                for h in range(2):
                    ctx.op("vector", [lambda e, h=h, nchs=nchs: e.reduce_sum(
                               den[:], mx[:, h * 32:h * 32 + nchs], axis=AX.X)],
                           reads=["mx"], writes=["den"], tag=f"den{jt}_{h}")
                    ctx.op("vector", [lambda e, h=h: e.reciprocal(rden[h][:],
                                                                  den[:])],
                           reads=["den"], writes=[f"rden{h}"], tag=f"rd{jt}_{h}")

                def make_tail(jt=jt, par=par):
                    ops = []
                    for h in range(2):
                        def ev_ot(h=h):
                            ctx.op("vector", [lambda e, h=h: e.tensor_copy(
                                       OTb[:, h * 128:(h + 1) * 128],
                                       pb[obank[h]][:, 0:128])],
                                   reads=[otk[h]], writes=["OTb"],
                                   tag=f"evOT{jt}_{h}")
                        ops.append(ev_ot)
                    for h in range(2):
                        for mc in range(4):
                            def rp(h=h, mc=mc):
                                ctx.op("tensor", [lambda e, mc=mc, h=h: e.matmul(
                                           pb[0][:, 0:M // 4],
                                           OTb[:, h * 128:(h + 1) * 128],
                                           waob[h][:, mc * (M // 4):
                                                    (mc + 1) * (M // 4)],
                                           start=True, stop=True)],
                                       reads=["OTb", f"waob{h}"], writes=["pbR0"],
                                       tag=f"rp{jt}_{h}_{mc}")
                            ops.append(rp)
                            if h == 0:
                                def evr(mc=mc):
                                    ctx.op("vector", [lambda e, mc=mc:
                                               e.tensor_scalar(
                                               rstgf[:, mc * (M // 4):
                                                     (mc + 1) * (M // 4)],
                                               pb[0][:, 0:M // 4],
                                               rden[0][:], None, ALU.mult)],
                                           reads=["pbR0", "rden0"], writes=["xn"],
                                           tag=f"evR0_{jt}_{mc}")
                            else:
                                def evr(mc=mc):
                                    ctx.op("vector",
                                           [lambda e, mc=mc:
                                            e.scalar_tensor_tensor(
                                               rstg[:, mc * (M // 4):
                                                    (mc + 1) * (M // 4)],
                                               pb[0][:, 0:M // 4],
                                               rden[1][:],
                                               rstgf[:, mc * (M // 4):
                                                     (mc + 1) * (M // 4)],
                                               ALU.mult, ALU.add)],
                                           reads=["pbR0", "rden1", "xn"],
                                           writes=["rstg"], tag=f"evR1_{jt}_{mc}")
                            ops.append(evr)
                    def st(jt=jt):
                        ctx.dma("gpsimd", rscr.ap()[jt * 128:(jt + 1) * 128, :],
                                rstg[:], reads=["rstg"], writes=["rscr"],
                                tag=f"stR{jt}")
                    ops.append(st)
                    return ops
                pending.extend(make_tail())
            drain_all()

            ctx.op("gpsimd",
                   [lambda e: e.collective_compute(
                       "ReduceScatter", ALU.add, replica_groups=[list(range(NC))],
                       ins=[rscr.ap().opt()], outs=[rsum.ap().opt()])],
                   reads=["rscr"], writes=["rsum"], sem="s_cc", tag="rs_r")
            for t in range(NT):
                ctx.dma("sync", rsb[:], rsum.ap()[t * 128:(t + 1) * 128, :],
                        reads=["rsum"], writes=["rstg"], tag=f"ldr{t}")
                xs_t = xs[:, t * M:(t + 1) * M]
                ctx.op("vector", [lambda e, xs_t=xs_t: e.tensor_add(xs_t, xs_t,
                                                                    rsb[:])],
                       reads=["rstg", f"xs{t}"], writes=[f"xs{t}"], tag=f"addr{t}")
        ctx.phase_barrier()

        with contextlib.ExitStack() as e3:
            sb, wk = alloc_ffn(e3, "_f2")
            emit_ffn(ctx, sb, wk, NT, M, I, NC, gsf, usf, dsf, hscr, tag="f2")
            for t in range(NT):
                ctx.dma("gpsimd", xout[t * 128:(t + 1) * 128, :],
                        xs[:, t * M:(t + 1) * M], reads=[f"xs{t}"], tag=f"stx{t}")
        for s in sorted(ctx.cnt):
            if s.startswith("qd_"):
                ctx._emit_wait("gpsimd", s, ctx.cnt[s])
        ctx.build(eg)
    global _last_ctx
    _last_ctx = ctx
    return nc


_last_ctx = None


_NC, _TPC, _M, _I, _D, _G, _H = 8, 1024, 2048, 8192, 128, 4, 4
_T = _NC * _TPC
_cache = {}


def _consts():
    if "c" not in _cache:
        F = np.float32
        half = 64
        pos = np.arange(_T, dtype=F)[:, None]
        freqs = np.power(F(10000.0), -(np.arange(half, dtype=F) / F(half)))[None, :]
        ang = (pos * freqs).astype(F)
        qkm = F(_D ** -0.25)
        cosT = np.ascontiguousarray((np.cos(ang).astype(F) * qkm).T)
        sinT = np.ascontiguousarray((np.sin(ang).astype(F) * qkm).T)
        import ml_dtypes
        idn = np.eye(128, dtype=np.float32)
        idnb = np.eye(128, dtype=ml_dtypes.bfloat16)
        i_ = np.arange(128)
        dmask = np.where(i_[:, None] < i_[None, :], F(-1e30), F(0.0)).astype(F)
        onesv = np.ones((128, 1), dtype=ml_dtypes.bfloat16)
        zmaskb = np.where(i_[:, None] < i_[None, :], 0.0, 1.0).astype(ml_dtypes.bfloat16)
        _cache["c"] = (cosT, sinT, idn, idnb, dmask, onesv, zmaskb)
    return _cache["c"]


def kernel(L0_attn_x, L0_attn_w_aq, L0_attn_w_ak, L0_attn_w_av, L0_attn_w_ao,
           L0_ffn_gate_w, L0_ffn_up_w, L0_ffn_down_w):
    cosT, sinT, idn, idnb, dmask, onesv, zmaskb = _consts()
    if "nc" not in _cache:
        _cache["nc"] = build_fused_program(_NC, _TPC, _M, _I, _D)
    nc = _cache["nc"]
    x = np.ascontiguousarray(np.asarray(L0_attn_x, dtype=np.float32).reshape(_T, _M))
    gw = np.asarray(L0_ffn_gate_w, dtype=np.float32)
    uw = np.asarray(L0_ffn_up_w, dtype=np.float32)
    dw = np.asarray(L0_ffn_down_w, dtype=np.float32)
    w_aq = np.asarray(L0_attn_w_aq, dtype=np.float32)
    w_ak = np.asarray(L0_attn_w_ak, dtype=np.float32)
    w_av = np.asarray(L0_attn_w_av, dtype=np.float32)
    w_ao = np.asarray(L0_attn_w_ao, dtype=np.float32)
    ISL = _I // _NC
    maps = []
    for c in range(_NC):
        g = c // 2
        h0, h1 = 2 * (c % 2), 2 * (c % 2) + 1
        maps.append({
            "x": np.ascontiguousarray(x[c * _TPC:(c + 1) * _TPC]),
            "gw": np.ascontiguousarray(gw[c * ISL:(c + 1) * ISL]),
            "uw": np.ascontiguousarray(uw[c * ISL:(c + 1) * ISL]),
            "dw": np.ascontiguousarray(dw[:, c * ISL:(c + 1) * ISL]),
            "wak": np.ascontiguousarray(w_ak[g]),
            "wav": np.ascontiguousarray(w_av[g]),
            "waq0": np.ascontiguousarray(w_aq[g, h0]),
            "waq1": np.ascontiguousarray(w_aq[g, h1]),
            "wao0": np.ascontiguousarray(w_ao[g, h0]),
            "wao1": np.ascontiguousarray(w_ao[g, h1]),
            "idn": idn, "idnb": idnb, "dmask": dmask,
            "onesv": onesv, "zmask": zmaskb,
            "cost": cosT, "sint": sinT,
        })
    res = run_bass_kernel_spmd(nc, maps, core_ids=list(range(_NC)))
    out = np.concatenate([res.results[i]["xout"] for i in range(_NC)], axis=0)
    return out.reshape(1, _T, _M)


# version 42
# speedup vs baseline: 1.0331x; 1.0331x over previous
import numpy as np
import contextlib
import concourse.bass as bass
import concourse.mybir as mybir
from concourse.bass_utils import run_bass_kernel_spmd

F32 = mybir.dt.float32
BF16 = mybir.dt.bfloat16
ALU = mybir.AluOpType
ACT = mybir.ActivationFunctionType
AX = mybir.AxisListType

ENGINES = ("sync", "vector", "scalar", "tensor", "gpsimd")


class Ctx:

    def __init__(self, nc):
        self.nc = nc
        self.prog = {e: [] for e in ENGINES}
        self.cnt = {}
        self.sem_handles = {}
        self.last_w = {}
        self.readers = {}
        self.waited = {e: {} for e in ENGINES}

    def sem(self, name):
        if name not in self.cnt:
            self.cnt[name] = 0
        return name

    def _emit_wait(self, eng, semname, val):
        if val <= 0:
            return
        if self.waited[eng].get(semname, -1) >= val:
            return
        self.waited[eng][semname] = val
        self.prog[eng].append(("wait", semname, val))

    def op(self, eng, fns, reads=(), writes=(), sem=None, inc=1, tag=""):
        if sem is None:
            sem = "s_" + eng
        self.sem(sem)
        waits = {}
        for b in reads:
            lw = self.last_w.get(b)
            if lw:
                s, c = lw
                waits[s] = max(waits.get(s, 0), c)
        for b in writes:
            lw = self.last_w.get(b)
            if lw:
                s, c = lw
                waits[s] = max(waits.get(s, 0), c)
            for s, c in self.readers.get(b, ()):
                waits[s] = max(waits.get(s, 0), c)
        for s, c in sorted(waits.items()):
            self._emit_wait(eng, s, c)
        self.cnt[sem] += inc
        cval = self.cnt[sem]
        self.prog[eng].append(("ops", fns, sem, inc, tag))
        for b in writes:
            self.last_w[b] = (sem, cval)
            self.readers[b] = []
        for b in reads:
            self.readers.setdefault(b, []).append((sem, cval))
        return cval

    def dma(self, eng, out, in_, reads=(), writes=(), tag=""):
        sem = "qd_" + (writes[0] if writes else "out")
        self._dma_rr = getattr(self, "_dma_rr", 0) + 1
        eng = ("sync", "scalar")[self._dma_rr % 2]
        return self.op(eng, [lambda e, o=out, i=in_: e.dma_start(out=o, in_=i)],
                       reads=reads, writes=writes, sem=sem, inc=16, tag=tag)

    def phase_barrier(self):
        for e in ENGINES:
            for s in sorted(self.cnt):
                self._emit_wait(e, s, self.cnt[s])
        self.last_w = {}
        self.readers = {}

    def build(self, es):
        nc = self.nc
        for name in self.cnt:
            self.sem_handles[name] = es.enter_context(nc.semaphore(name))
        block = es.enter_context(nc.Block())
        for engname in ENGINES:
            items = self.prog[engname]
            if not items:
                continue

            def make(engname, items):
                def run(eng):
                    for it in items:
                        if it[0] == "wait":
                            _, s, v = it
                            eng.wait_ge(self.sem_handles[s], v)
                        else:
                            _, fns, s, inc, tag = it
                            inst = None
                            for fn in fns:
                                inst = fn(eng)
                            inst.then_inc(self.sem_handles[s], inc)
                return run

            getattr(block, engname)(make(engname, items))


MAGIC_CLAMP = 1.5 * 2.0 ** 14
MAGIC_SCALE = float(2.0 ** 20)


def emit_quant(ctx, W, src_key, src_ap, out_key, out_ap, wk, tag=""):
    nb = W // 32
    b3 = lambda ap: ap.rearrange("p (n b) -> p n b", b=32)
    bmt, st, rst, vt, Mt, at = wk["bm"], wk["s"], wk["rs"], wk["v"], wk["M"], wk["a"]
    kq = wk["kq"]
    kbm, ks, krs, kv, kM, ka = (kq + x for x in ("bm", "s", "rs", "v", "M", "a"))
    ctx.op("vector", [lambda e: e.reduce_max(bmt[:, 0:nb], b3(src_ap), axis=AX.X,
                                             apply_absolute_value=True)],
           reads=[src_key], writes=[kbm], tag=tag + "/bmax")
    ctx.op("vector", [lambda e: e.tensor_scalar(st[:, 0:nb], bmt[:, 0:nb], 1.0 / 448.0,
                                                1e-30, ALU.mult, ALU.max)],
           reads=[kbm], writes=[ks], tag=tag + "/s")
    ctx.op("vector", [lambda e: e.reciprocal(rst[:, 0:nb], st[:, 0:nb])],
           reads=[ks], writes=[krs], tag=tag + "/rs")
    rsb = rst[:, 0:nb].unsqueeze(2).broadcast_to([128, nb, 32])
    ctx.op("vector", [lambda e: e.tensor_tensor(b3(vt[:, 0:W]), b3(src_ap), rsb,
                                                ALU.mult)],
           reads=[krs, src_key], writes=[kv], tag=tag + "/v")
    ctx.op("scalar", [lambda e: e.activation(Mt[:, 0:W], vt[:, 0:W], ACT.Abs,
                                             scale=MAGIC_SCALE)],
           reads=[kv], writes=[kM], tag=tag + "/absM")
    ctx.op("vector", [lambda e: e.tensor_scalar_max(Mt[:, 0:W], Mt[:, 0:W],
                                                    MAGIC_CLAMP)],
           reads=[kM], writes=[kM], tag=tag + "/clamp")
    ctx.op("vector", [lambda e: e.tensor_add(at[:, 0:W], vt[:, 0:W], Mt[:, 0:W])],
           reads=[kv, kM], writes=[ka], tag=tag + "/add")
    ctx.op("vector", [lambda e: e.tensor_sub(at[:, 0:W], at[:, 0:W], Mt[:, 0:W])],
           reads=[ka, kM], writes=[ka], tag=tag + "/sub")
    sb_ = st[:, 0:nb].unsqueeze(2).broadcast_to([128, nb, 32])
    ctx.op("vector", [lambda e: e.tensor_tensor(b3(out_ap), b3(at[:, 0:W]), sb_,
                                                ALU.mult)],
           reads=[ka, ks], writes=[out_key], tag=tag + "/deq")


def emit_ln(ctx, W, src_key, src_ap, out_key, out_ap, wk, tag=""):
    inv = 1.0 / W
    xsum, xsq, junk, mean, var, rstd, m2 = (wk[x] for x in
        ("xsum", "xsq", "junk", "mean", "var", "rstd", "m2"))
    K = wk["kl"]
    jk = wk["junk_key"]
    ctx.op("vector", [lambda e: e.reduce_sum(xsum[:], src_ap, axis=AX.X)],
           reads=[src_key], writes=[K + "xsum"], tag=tag + "/sum")
    ctx.op("scalar", [lambda e: e.activation(junk[:, 0:W], src_ap, ACT.Square,
                                             accum_out=xsq[:])],
           reads=[src_key], writes=[K + "xsq", jk], tag=tag + "/sumsq")
    ctx.op("vector", [lambda e: e.tensor_scalar_mul(mean[:], xsum[:], inv)],
           reads=[K + "xsum"], writes=[K + "mean"], tag=tag + "/mean")
    ctx.op("vector", [lambda e: e.tensor_scalar_mul(var[:], xsq[:], inv)],
           reads=[K + "xsq"], writes=[K + "var"], tag=tag + "/e2")
    ctx.op("vector", [lambda e: e.tensor_mul(m2[:], mean[:], mean[:])],
           reads=[K + "mean"], writes=[K + "m2"], tag=tag + "/m2")
    ctx.op("vector", [lambda e: e.tensor_sub(var[:], var[:], m2[:])],
           reads=[K + "var", K + "m2"], writes=[K + "var"], tag=tag + "/var")
    ctx.op("vector", [lambda e: e.tensor_scalar_add(var[:], var[:], 1e-5)],
           reads=[K + "var"], writes=[K + "var"], tag=tag + "/eps")
    ctx.op("scalar", [lambda e: e.activation(rstd[:], var[:], ACT.Sqrt)],
           reads=[K + "var"], writes=[K + "rstd"], tag=tag + "/sqrt")
    ctx.op("vector", [lambda e: e.reciprocal(rstd[:], rstd[:])],
           reads=[K + "rstd"], writes=[K + "rstd"], tag=tag + "/rsqrt")
    W_ = src_ap.shape[-1]
    rstd_b = rstd[:].broadcast_to([128, W_])
    ctx.op("vector", [lambda e: e.scalar_tensor_tensor(out_ap, src_ap, mean[:],
                                                       rstd_b, ALU.subtract, ALU.mult)],
           reads=[src_key, K + "rstd", K + "mean"], writes=[out_key],
           tag=tag + "/apply")


def emit_weight_prep(ctx, t_in, t_sl, n_tiles, Wfree, n_tr, wk, sb, tag):
    wt, dqw, stg, idn = sb["wt"], sb["dqw"], sb["stg"], sb["idn"]
    for r in range(n_tiles):
        ctx.dma("sync", wt[:, 0:Wfree], t_in[r * 128:(r + 1) * 128, :],
                writes=["wt"], tag=f"{tag}/ld{r}")
        emit_quant(ctx, Wfree, "wt", wt[:, 0:Wfree], "dqw", dqw[:, 0:Wfree],
                   wk, tag=f"{tag}/q{r}")
        for j in range(n_tr):
            pbk = "pbt" + str(j % 2)
            pbt = sb[pbk]
            ctx.op("tensor",
                   [lambda e, j=j, pbt=pbt: e.transpose(pbt[:, 0:128],
                        dqw[:, j * 128:(j + 1) * 128], idn[:])],
                   reads=["dqw", "idn"], writes=[pbk], tag=f"{tag}/tr{r}_{j}")
            ctx.op("vector",
                   [lambda e, j=j, pbt=pbt: e.tensor_copy(
                       stg[:, j * 128:(j + 1) * 128], pbt[:, 0:128])],
                   reads=[pbk], writes=["stg"], tag=f"{tag}/ev{r}_{j}")
        ctx.dma("sync", t_sl[:, :, r * 128:(r + 1) * 128].transpose([1, 0, 2]),
                stg[:, 0:n_tr * 128],
                reads=["stg"], writes=[f"{tag}_sl"], tag=f"{tag}/st{r}")


def emit_ffn(ctx, sb, wk, NT, M, I, NC, gsf, usf, dsf, hscr, tag="ffn"):
    MCH = M // 128
    IC = I // 512
    ISL = I // NC
    xs, xT = sb["xs"], sb["xT"]
    xn, dqx = sb["xn"], sb["dqw"]
    gsb, hh, dqh, hTstg = sb["gsb"], sb["hh"], sb["dqh"], sb["hTstg"]

    idn = sb["idn"]
    pb = sb["pb"]
    for t in range(NT):
        xs_t = xs[:, t * M:(t + 1) * M]
        emit_ln(ctx, M, f"xs{t}", xs_t, "xn", xn[:], wk, tag=f"{tag}/ln{t}")
        emit_quant(ctx, M, "xn", xn[:], "dqw", dqx[:], wk, tag=f"{tag}/qx{t}")
        for j in range(MCH):
            pbk = "pbt" + str(j % 2)
            pbt = sb[pbk]
            ctx.op("tensor",
                   [lambda e, j=j, pbt=pbt: e.transpose(pbt[:, 0:128],
                        dqx[:, j * 128:(j + 1) * 128], idn[:])],
                   reads=["dqw", "idn"], writes=[pbk], tag=f"{tag}/xtr{t}_{j}")
            ctx.op("vector",
                   [lambda e, t=t, j=j, pbt=pbt: e.tensor_copy(
                       xT[:, t * M + j * 128: t * M + (j + 1) * 128], pbt[:, 0:128])],
                   reads=[pbk], writes=["xT"], tag=f"{tag}/xev{t}_{j}")
    NHC = I // 256
    TPC_ = NT * 128
    kqp = wk["kq"]
    ctx.op("vector", [lambda e: e.memset(wk["hbm"][:, 0:16], 0.0)],
           reads=["wt"],
           writes=["wt"] + [kqp + f"h{p}_{x}" for p in (0, 1)
                            for x in ("v", "M", "a")],
           tag=f"{tag}/hqbar")
    pending = []
    def flush_pending(ctx):
        for fn in pending:
            fn()
        pending.clear()
    for c in range(NHC):
        rank, off = (c * 256) // ISL, (c * 256) % ISL
        pw = c % 2
        wgb, wgk = (sb["wg0"], "wg0") if pw == 0 else (sb["wg1"], "wg1")
        wub, wuk = (sb["wu0"], "wu0") if pw == 0 else (sb["wu1"], "wu1")
        qg = ("sync", "scalar", "gpsimd")[c % 3]
        qu = ("scalar", "gpsimd", "sync")[c % 3]
        ctx.dma(qg, wgb[:], gsf[rank, :, :, off:off + 256].transpose([1, 0, 2]),
                reads=["wsf"], writes=[wgk], tag=f"{tag}/wg{c}")
        ctx.dma(qu, wub[:], usf[rank, :, :, off:off + 256].transpose([1, 0, 2]),
                reads=["wsf"], writes=[wuk], tag=f"{tag}/wu{c}")
        for t in range(NT):
            gb = (0, 2, 4)[t % 3]
            ub = (1, 3, 5)[t % 3]
            gate_fns = [(lambda e, t=t, j=j, wgb=wgb, gb=gb: e.matmul(
                            pb[gb][:, 0:256],
                            xT[:, t * M + j * 128: t * M + (j + 1) * 128],
                            wgb[:, j * 256:(j + 1) * 256],
                            start=(j == 0), stop=(j == MCH - 1)))
                        for j in range(MCH)]
            ctx.op("tensor", gate_fns, reads=["xT", wgk], writes=[f"pb{gb}"],
                   tag=f"{tag}/mmg{c}_{t}")
            up_fns = [(lambda e, t=t, j=j, wub=wub, ub=ub: e.matmul(
                          pb[ub][:, 0:256],
                          xT[:, t * M + j * 128: t * M + (j + 1) * 128],
                          wub[:, j * 256:(j + 1) * 256],
                          start=(j == 0), stop=(j == MCH - 1)))
                      for j in range(MCH)]
            ctx.op("tensor", up_fns, reads=["xT", wuk], writes=[f"pb{ub}"],
                   tag=f"{tag}/mmu{c}_{t}")
            while len(pending) > 1:
                pending.pop(0)()
            p2 = t % 2
            off2 = p2 * 256
            wk2 = dict(wk)
            wk2["kq"] = wk["kq"] + f"h{p2}_"
            wk2["bm"] = wk["hbm"][:, p2 * 8:p2 * 8 + 8]
            wk2["s"] = wk["hs"][:, p2 * 8:p2 * 8 + 8]
            wk2["rs"] = wk["hrs"][:, p2 * 8:p2 * 8 + 8]
            wk2["v"] = wk["hv"][:, off2:off2 + 256]
            wk2["M"] = wk["hM"][:, off2:off2 + 256]
            wk2["a"] = wk["ha"][:, off2:off2 + 256]
            ctx.op("scalar", [lambda e, gb=gb, off2=off2: e.activation(
                       gsb[:, off2:off2 + 256], pb[gb][:, 0:256], ACT.Silu)],
                   reads=[f"pb{gb}"], writes=[f"gsb{p2}"],
                   tag=f"{tag}/silu{c}_{t}")
            ctx.op("vector", [lambda e, ub=ub, off2=off2: e.tensor_mul(
                       hh[:, off2:off2 + 256], gsb[:, off2:off2 + 256],
                       pb[ub][:, 0:256])],
                   reads=[f"gsb{p2}", f"pb{ub}"], writes=[f"hh{p2}"],
                   tag=f"{tag}/hmul{c}_{t}")
            emit_quant(ctx, 256, f"hh{p2}", hh[:, off2:off2 + 256],
                       f"dqh{p2}", dqh[:, off2:off2 + 256], wk2,
                       tag=f"{tag}/qh{c}_{t}")

            def make_tail(c=c, t=t, p2=p2, off2=off2):
                def tail():
                    for s in range(2):
                        pbk = "pbt" + str(s % 2)
                        pbt = sb[pbk]
                        ctx.op("tensor",
                               [lambda e, s=s, pbt=pbt, off2=off2: e.transpose(
                                    pbt[:, 0:128],
                                    dqh[:, off2 + s * 128:off2 + (s + 1) * 128],
                                    idn[:])],
                               reads=[f"dqh{p2}", "idn"], writes=[pbk],
                               tag=f"{tag}/htr{c}_{t}_{s}")
                        ctx.op("vector",
                               [lambda e, s=s, t=t, pbt=pbt: e.tensor_copy(
                                   hTstg[:, s * TPC_ + t * 128:
                                         s * TPC_ + (t + 1) * 128], pbt[:, 0:128])],
                               reads=[pbk], writes=["hTstg"],
                               tag=f"{tag}/hev{c}_{t}_{s}")
                    if t == NT - 1:
                        for s in range(2):
                            ctx.dma("sync", hscr[c * 2 + s, :, :],
                                    hTstg[:, s * TPC_:(s + 1) * TPC_],
                                    reads=["hTstg"], writes=["hscr"],
                                    tag=f"{tag}/hst{c}_{s}")
                return tail
            pending.append(make_tail())
    flush_pending(ctx)
    KCH = I // 128
    HPW = M // 4
    for hp in range(4):
        for kc in range(KCH):
            rank, idx = (kc * 128) // ISL, ((kc * 128) % ISL) // 128
            wdb = sb[f"wd{kc % 4}"]
            wdk = f"wd{kc % 4}"
            htb = sb[f"ht{kc % 4}"]
            htk = f"ht{kc % 4}"
            ctx.dma(("sync", "scalar", "gpsimd")[kc % 3], wdb[:, 0:HPW],
                    dsf[rank, idx, :, hp * HPW:(hp + 1) * HPW],
                    reads=["wsf"], writes=[wdk], tag=f"{tag}/wd{hp}_{kc}")
            ctx.dma(("gpsimd", "sync", "scalar")[kc % 3], htb[:], hscr[kc, :, :],
                    reads=["hscr"], writes=[htk], tag=f"{tag}/htl{hp}_{kc}")
            mm_fns = [(lambda e, t=t, kc=kc, htb=htb, wdb=wdb: e.matmul(
                          pb[t][:, 0:HPW], htb[:, t * 128:(t + 1) * 128],
                          wdb[:, 0:HPW],
                          start=(kc == 0), stop=(kc == KCH - 1)))
                      for t in range(NT)]
            ctx.op("tensor", mm_fns, reads=[wdk, htk],
                   writes=[f"pbB{t}" for t in range(NT)], tag=f"{tag}/mmd{hp}_{kc}")
        for t in range(NT):
            xs_sl = xs[:, t * M + hp * HPW: t * M + (hp + 1) * HPW]
            ctx.op("vector",
                   [lambda e, t=t, xs_sl=xs_sl: e.scalar_tensor_tensor(
                       xs_sl, pb[t][:, 0:HPW], 0.5, xs_sl, ALU.mult, ALU.add)],
                   reads=[f"pbB{t}", f"xs{t}"], writes=[f"xs{t}"],
                   tag=f"{tag}/acc{hp}_{t}")


def emit_rope_evac(ctx, psum, psum_key, out_ap, out_key, N, cosb, sinb, tmp1, tmp2,
                   tag):
    pe_ap, po_ap = psum[0:64, 0:N], psum[64:128, 0:N]
    ctx.op("vector", [lambda e: e.tensor_mul(tmp1[0:64, 0:N], pe_ap, cosb)],
           reads=[psum_key, "ropetbl"], writes=["rtmp1"], tag=tag + "/ec")
    ctx.op("vector", [lambda e: e.tensor_mul(tmp2[0:64, 0:N], po_ap, sinb)],
           reads=[psum_key, "ropetbl"], writes=["rtmp2"], tag=tag + "/os")
    ctx.op("vector", [lambda e: e.tensor_sub(out_ap[0:64, 0:N], tmp1[0:64, 0:N],
                                             tmp2[0:64, 0:N])],
           reads=["rtmp1", "rtmp2"], writes=[out_key], tag=tag + "/re")
    ctx.op("vector", [lambda e: e.tensor_mul(tmp1[0:64, 0:N], pe_ap, sinb)],
           reads=[psum_key, "ropetbl"], writes=["rtmp1"], tag=tag + "/es")
    ctx.op("vector", [lambda e: e.tensor_mul(tmp2[0:64, 0:N], po_ap, cosb)],
           reads=[psum_key, "ropetbl"], writes=["rtmp2"], tag=tag + "/oc")
    ctx.op("vector", [lambda e: e.tensor_add(out_ap[64:128, 0:N], tmp1[0:64, 0:N],
                                             tmp2[0:64, 0:N])],
           reads=["rtmp1", "rtmp2"], writes=[out_key], tag=tag + "/ro")


def build_fused_program(NC=8, TPC=1024, M=2048, I=8192, D=128):
    T = NC * TPC
    NT = TPC // 128
    NTG = T // 128
    MCH = M // 128
    ISL = I // NC
    nc = bass.Bass("TRN2", target_bir_lowering=False)
    x_in = nc.dram_tensor("x", [TPC, M], F32, kind="ExternalInput").ap()
    gw = nc.dram_tensor("gw", [ISL, M], F32, kind="ExternalInput").ap()
    uw = nc.dram_tensor("uw", [ISL, M], F32, kind="ExternalInput").ap()
    dw = nc.dram_tensor("dw", [M, ISL], F32, kind="ExternalInput").ap()
    wak = nc.dram_tensor("wak", [M, D], F32, kind="ExternalInput").ap()
    wav = nc.dram_tensor("wav", [M, D], F32, kind="ExternalInput").ap()
    waq0 = nc.dram_tensor("waq0", [M, D], F32, kind="ExternalInput").ap()
    waq1 = nc.dram_tensor("waq1", [M, D], F32, kind="ExternalInput").ap()
    wao0 = nc.dram_tensor("wao0", [D, M], F32, kind="ExternalInput").ap()
    wao1 = nc.dram_tensor("wao1", [D, M], F32, kind="ExternalInput").ap()
    idn_in = nc.dram_tensor("idn", [128, 128], F32, kind="ExternalInput").ap()
    idnb_in = nc.dram_tensor("idnb", [128, 128], BF16, kind="ExternalInput").ap()
    dmask_in = nc.dram_tensor("dmask", [128, 128], F32, kind="ExternalInput").ap()
    ones_in = nc.dram_tensor("onesv", [128, 1], BF16, kind="ExternalInput").ap()
    zmask_in = nc.dram_tensor("zmask", [128, 128], BF16, kind="ExternalInput").ap()
    cos_in = nc.dram_tensor("cost", [64, T], F32, kind="ExternalInput").ap()
    sin_in = nc.dram_tensor("sint", [64, T], F32, kind="ExternalInput").ap()
    xout = nc.dram_tensor("xout", [TPC, M], F32, kind="ExternalOutput").ap()
    gsl = nc.dram_tensor("gsl", [MCH, 128, ISL], BF16)
    usl = nc.dram_tensor("usl", [MCH, 128, ISL], BF16)
    dsl = nc.dram_tensor("dsl", [ISL // 128, 128, M], BF16)
    gsf_t = nc.dram_tensor("gsf", [NC, MCH, 128, ISL], BF16)
    usf_t = nc.dram_tensor("usf", [NC, MCH, 128, ISL], BF16)
    dsf_t = nc.dram_tensor("dsf", [NC, ISL // 128, 128, M], BF16)
    hscr_t = nc.dram_tensor("hscr", [I // 128, 128, TPC], BF16)
    xnsl = nc.dram_tensor("xnsl", [MCH, 128, TPC], BF16)
    xnsf = nc.dram_tensor("xnsf", [NC, MCH, 128, TPC], BF16)
    rscr = nc.dram_tensor("rscr", [T, M], BF16)
    rsum = nc.dram_tensor("rsum", [TPC, M], BF16)
    gsf, usf, dsf, hscr = gsf_t.ap(), usf_t.ap(), dsf_t.ap(), hscr_t.ap()

    ctx = Ctx(nc)
    with contextlib.ExitStack() as eg:
        E = eg.enter_context
        xs = E(nc.sbuf_tensor("xs", [128, NT * M], F32))
        idn = E(nc.sbuf_tensor("idn_sb", [128, 128], F32))
        idnb = E(nc.sbuf_tensor("idnb_sb", [128, 128], BF16))
        dmask = E(nc.sbuf_tensor("dmask_sb", [128, 128], F32))
        lnw = {"kl": "ln_", "kq": "q_"}
        for n in ("xsum", "xsq", "mean", "var", "rstd", "m2"):
            lnw[n] = E(nc.sbuf_tensor("ln_" + n, [128, 1], F32))
        nm = [E(nc.sbuf_tensor(f"nm{h}", [128, 1], F32)) for h in range(2)]
        den = E(nc.sbuf_tensor("den", [128, 1], F32))
        rden = [E(nc.sbuf_tensor(f"rden{h}", [128, 1], F32)) for h in range(2)]
        mx = E(nc.sbuf_tensor("mx", [128, 64], F32))
        OTb = E(nc.sbuf_tensor("OTb", [128, 256], BF16))

        ctx.op("vector", [lambda e: e.memset(mx[:], 0.0)], writes=["mx"],
               tag="mxinit")
        ctx.dma("sync", idn[:], idn_in, writes=["idn"], tag="idn")
        ctx.dma("sync", idnb[:], idnb_in, writes=["idnb"], tag="idnb")
        ctx.dma("sync", dmask[:], dmask_in, writes=["dmask"], tag="dmask")

        def alloc_ffn(es, sfx):
            sb = {"xs": xs, "idn": idn}
            EE = es.enter_context
            sb["xT"] = EE(nc.sbuf_tensor("xT" + sfx, [128, NT * M], BF16))
            sb["wt"] = EE(nc.sbuf_tensor("wt" + sfx, [128, M], F32))
            sb["dqw"] = EE(nc.sbuf_tensor("dqw" + sfx, [128, M], F32))
            sb["stg"] = EE(nc.sbuf_tensor("stg" + sfx, [128, MCH * 128], BF16))
            sb["xn"] = EE(nc.sbuf_tensor("xn" + sfx, [128, M], F32))
            sb["wg0"] = EE(nc.sbuf_tensor("wg0" + sfx, [128, MCH * 256], BF16))
            sb["wg1"] = EE(nc.sbuf_tensor("wg1" + sfx, [128, MCH * 256], BF16))
            sb["wu0"] = EE(nc.sbuf_tensor("wu0" + sfx, [128, MCH * 256], BF16))
            sb["wu1"] = EE(nc.sbuf_tensor("wu1" + sfx, [128, MCH * 256], BF16))
            sb["gsb"] = EE(nc.sbuf_tensor("gsb" + sfx, [128, 512], F32))
            sb["hh"] = EE(nc.sbuf_tensor("hh" + sfx, [128, 512], F32))
            sb["dqh"] = EE(nc.sbuf_tensor("dqh" + sfx, [128, 512], F32))
            sb["hTstg"] = EE(nc.sbuf_tensor("hTstg" + sfx, [128, 2 * NT * 128], BF16))
            for i4 in range(4):
                sb[f"wd{i4}"] = EE(nc.sbuf_tensor(f"wd{i4}" + sfx, [128, 512], BF16))
                sb[f"ht{i4}"] = EE(nc.sbuf_tensor(f"ht{i4}" + sfx, [128, TPC], BF16))
            wk = dict(lnw)
            wk["bm"] = EE(nc.sbuf_tensor("q_bm" + sfx, [128, M // 32], F32))
            wk["s"] = EE(nc.sbuf_tensor("q_s" + sfx, [128, M // 32], F32))
            wk["rs"] = EE(nc.sbuf_tensor("q_rs" + sfx, [128, M // 32], F32))
            wk["v"] = EE(nc.sbuf_tensor("q_v" + sfx, [128, M], F32))
            wk["M"] = EE(nc.sbuf_tensor("q_M" + sfx, [128, M], F32))
            wk["a"] = EE(nc.sbuf_tensor("q_a" + sfx, [128, M], F32))
            wk["junk"] = wk["v"]
            wk["junk_key"] = "q_v"
            wk["hbm"] = EE(nc.sbuf_tensor("qh_bm" + sfx, [128, 16], F32))
            wk["hs"] = EE(nc.sbuf_tensor("qh_s" + sfx, [128, 16], F32))
            wk["hrs"] = EE(nc.sbuf_tensor("qh_rs" + sfx, [128, 16], F32))
            if M >= 1024:
                wk["hv"] = sb["wt"][:, 0:512]
                wk["hM"] = sb["wt"][:, 512:1024]
            else:
                wk["hv"] = EE(nc.sbuf_tensor("qh_v" + sfx, [128, 512], F32))
                wk["hM"] = EE(nc.sbuf_tensor("qh_M" + sfx, [128, 512], F32))
            wk["ha"] = wk["hv"]
            pb = [EE(nc.psum_tensor(f"pb{i}" + sfx, [128, 512], F32))
                  for i in range(8)]
            sb["pb"] = pb
            sb["pbt0"] = pb[6]
            sb["pbt1"] = pb[7]
            return sb, wk

        with contextlib.ExitStack() as e1:
            sb, wk = alloc_ffn(e1, "_f1")
            for t in range(NT):
                ctx.dma("gpsimd", xs[:, t * M:(t + 1) * M],
                        x_in[t * 128:(t + 1) * 128, :], writes=[f"xs{t}"],
                        tag=f"ldx{t}")
            for t_in, t_sl, t_sf, nt_, wf_, ntr_, k in (
                    (gw, gsl, gsf_t, ISL // 128, M, MCH, "g"),
                    (uw, usl, usf_t, ISL // 128, M, MCH, "u"),
                    (dw, dsl, dsf_t, MCH, ISL, ISL // 128, "d")):
                emit_weight_prep(ctx, t_in, t_sl.ap(), nt_, wf_, ntr_, wk, sb, k)
                ctx.op("gpsimd",
                       [lambda e, t_sl=t_sl, t_sf=t_sf: e.collective_compute(
                           "AllGather", ALU.bypass,
                           replica_groups=[list(range(NC))],
                           ins=[t_sl.ap().opt()], outs=[t_sf.ap().opt()])],
                       reads=[f"{k}_sl"], writes=["wsf"], sem="s_cc",
                       tag=f"ag_{k}")
            emit_ffn(ctx, sb, wk, NT, M, I, NC, gsf, usf, dsf, hscr, tag="f1")
        ctx.phase_barrier()

        with contextlib.ExitStack() as e2:
            EE = e2.enter_context
            asb = {"xs": xs, "idn": idn}
            asb["xn"] = EE(nc.sbuf_tensor("xn_at", [128, M], F32))
            asb["stg"] = EE(nc.sbuf_tensor("stg_at", [128, MCH * 128], BF16))
            wakb = EE(nc.sbuf_tensor("wakb", [128, MCH * 128], BF16))
            wavb = EE(nc.sbuf_tensor("wavb", [128, MCH * 128], BF16))
            waqb = [EE(nc.sbuf_tensor(f"waqb{h}", [128, MCH * 128], BF16))
                    for h in range(2)]
            waob = [EE(nc.sbuf_tensor(f"waob{h}", [128, M], BF16)) for h in range(2)]
            KT = EE(nc.sbuf_tensor("KT", [128, T], BF16))
            Vb = EE(nc.sbuf_tensor("Vb", [128, T], BF16))
            xch = EE(nc.sbuf_tensor("xch", [128, MCH * 512], BF16))
            xqch = xch
            QTt = [EE(nc.sbuf_tensor(f"QTt{h}", [128, 256], BF16)) for h in range(2)]
            cosb = EE(nc.sbuf_tensor("cosb", [64, 512], F32))
            sinb = EE(nc.sbuf_tensor("sinb", [64, 512], F32))
            rtmp1 = EE(nc.sbuf_tensor("rtmp1", [64, 512], F32))
            rtmp2 = EE(nc.sbuf_tensor("rtmp2", [64, 512], F32))
            Pb = EE(nc.sbuf_tensor("Pb", [128, 6 * 512], BF16))
            q2b = EE(nc.sbuf_tensor("q2b", [128, 256], BF16))
            mqs = EE(nc.sbuf_tensor("mqs", [128, 4], F32))
            nmb = EE(nc.sbuf_tensor("nmb", [128, 4], F32))
            maxk2b = EE(nc.sbuf_tensor("maxk2b", [128, 1], F32))
            mxk = EE(nc.sbuf_tensor("mxk", [128, 17], F32))
            sc1b = EE(nc.sbuf_tensor("sc1b", [1, 1], BF16))
            ones_sb = EE(nc.sbuf_tensor("ones_sb", [128, 1], BF16))
            zmask = EE(nc.sbuf_tensor("zmask_sb", [128, 128], BF16))
            PTstgH = [EE(nc.sbuf_tensor(f"PTstg{h}", [128, 1024], BF16))
                      for h in range(2)]
            rstg = EE(nc.sbuf_tensor("rstg", [128, M], BF16))
            rstgf = asb["xn"]
            rsb = rstg
            awk = dict(lnw)
            awk["junk"] = asb["xn"]
            awk["junk_key"] = "xn"
            pb = [EE(nc.psum_tensor(f"apb{i}", [128, 512], F32))
                  for i in (0, 2, 3, 4, 5, 6)]
            pb = {i: t for i, t in zip((0, 2, 3, 4, 5, 6), pb)}
            ptpsH = [EE(nc.psum_tensor(f"ptps{h}", [128, 1024], BF16))
                     for h in range(2)]
            asb["pbt0"] = pb[5]
            asb["pbt1"] = pb[6]

            wtmp = asb["xn"]
            for src, dstb, k in ((wak, wakb, "wakb"), (wav, wavb, "wavb"),
                                 (waq0, waqb[0], "waqb0"), (waq1, waqb[1], "waqb1")):
                ctx.dma("sync", wtmp[:], src.rearrange("(j p) d -> p j d", p=128),
                        writes=["xn"], tag=f"ldw_{k}")
                ctx.op("vector", [lambda e, dstb=dstb: e.tensor_copy(dstb[:],
                                                                     wtmp[:])],
                       reads=["xn"], writes=[k], tag=f"cast_{k}")
            for h in range(2):
                src = wao0 if h == 0 else wao1
                ctx.dma("sync", wtmp[:, 0:M], src, writes=["xn"], tag=f"ldwao{h}")
                ctx.op("vector", [lambda e, h=h: e.tensor_copy(waob[h][:],
                                                               wtmp[:, 0:M])],
                       reads=["xn"], writes=[f"waob{h}"], tag=f"cast_wao{h}")

            ctx.dma("sync", ones_sb[:], ones_in, writes=["ones"], tag="ldones")
            ctx.dma("sync", zmask[:], zmask_in, writes=["zmask"], tag="ldzmask")
            for t in range(NT):
                xs_t = xs[:, t * M:(t + 1) * M]
                emit_ln(ctx, M, f"xs{t}", xs_t, "xn", asb["xn"][:], awk,
                        tag=f"lnT/ln{t}")
                for j in range(MCH):
                    pbk = "pbt" + str(j % 2)
                    pbt = asb[pbk]
                    ctx.op("tensor",
                           [lambda e, j=j, pbt=pbt: e.transpose(pbt[:, 0:128],
                                asb["xn"][:, j * 128:(j + 1) * 128], idn[:])],
                           reads=["xn", "idn"], writes=[pbk], tag=f"lnT/tr{t}_{j}")
                    ctx.op("vector",
                           [lambda e, j=j, pbt=pbt: e.tensor_copy(
                               asb["stg"][:, j * 128:(j + 1) * 128], pbt[:, 0:128])],
                           reads=[pbk], writes=["stg"], tag=f"lnT/ev{t}_{j}")
                ctx.dma("sync", xnsl.ap()[:, :, t * 128:(t + 1) * 128].transpose([1, 0, 2]),
                        asb["stg"][:, 0:MCH * 128],
                        reads=["stg"], writes=["xnsl"], tag=f"lnT/st{t}")
            ctx.op("gpsimd",
                   [lambda e: e.collective_compute(
                       "AllGather", ALU.bypass, replica_groups=[list(range(NC))],
                       ins=[xnsl.ap().opt()], outs=[xnsf.ap().opt()])],
                   reads=["xnsl"], writes=["xnsf"], sem="s_cc", tag="ag_xn")

            xnsf_ap = xnsf.ap()
            for tc in range(T // 512):
                rank, off = tc // (TPC // 512), (tc % (TPC // 512)) * 512
                ctx.dma("sync", xch[:],
                        xnsf_ap[rank, :, :, off:off + 512].transpose([1, 0, 2]),
                        reads=["xnsf"], writes=["xch"], tag=f"xch{tc}")
                ctx.dma("gpsimd", cosb[:], cos_in[:, tc * 512:(tc + 1) * 512],
                        writes=["ropetbl"], tag=f"cos{tc}")
                ctx.dma("gpsimd", sinb[:], sin_in[:, tc * 512:(tc + 1) * 512],
                        writes=["ropetbl"], tag=f"sin{tc}")
                fns = [(lambda e, j=j: e.matmul(pb[0][:],
                            wakb[:, j * 128:(j + 1) * 128],
                            xch[:, j * 512:(j + 1) * 512],
                            start=(j == 0), stop=(j == MCH - 1)))
                       for j in range(MCH)]
                ctx.op("tensor", fns, reads=["wakb", "xch"], writes=["apb0"],
                       tag=f"mmK{tc}")
                emit_rope_evac(ctx, pb[0], "apb0", KT[:, tc * 512:(tc + 1) * 512],
                               "KT", 512, cosb[:, 0:512], sinb[:, 0:512],
                               rtmp1, rtmp2, f"ropeK{tc}")
                for s in range(4):
                    fns = [(lambda e, j=j, s=s: e.matmul(
                                pb[3][:, s * 128:(s + 1) * 128],
                                xch[:, j * 512 + s * 128: j * 512 + (s + 1) * 128],
                                wavb[:, j * 128:(j + 1) * 128],
                                start=(j == 0), stop=(j == MCH - 1)))
                           for j in range(MCH)]
                    ctx.op("tensor", fns, reads=["wavb", "xch"], writes=["apb3"],
                           tag=f"mmV{tc}_{s}")
                ctx.op("vector", [lambda e, tc=tc: e.tensor_copy(
                           Vb[:, tc * 512:(tc + 1) * 512], pb[3][:])],
                       reads=["apb3"], writes=["Vb"], tag=f"evV{tc}")

            rbanks = [0, 1, 5, 6]
            ksq = Pb
            ctx.op("vector", [lambda e: e.memset(mxk[:], 0.0)], writes=["mxk"],
                   tag="mxkinit")
            for tc in range(T // 512):
                ctx.op("vector", [lambda e, tc=tc: e.tensor_mul(
                           ksq[:, 0:512], KT[:, tc * 512:(tc + 1) * 512],
                           KT[:, tc * 512:(tc + 1) * 512])],
                       reads=["KT"], writes=["Pb0"], tag=f"ksq{tc}")
                ctx.op("tensor", [lambda e: e.matmul(pb[3][0:1, 0:512], ones_sb[:],
                                                     ksq[:, 0:512], start=True,
                                                     stop=True)],
                       reads=["Pb0", "ones"], writes=["apb3"], tag=f"kn{tc}")
                ctx.op("vector", [lambda e, tc=tc: e.reduce_max(
                           mxk[0:1, tc:tc + 1], pb[3][0:1, 0:512], axis=AX.X)],
                       reads=["apb3"], writes=["mxk"], tag=f"knm{tc}")
            ctx.op("vector", [lambda e: e.reduce_max(mxk[0:1, 16:17],
                                                     mxk[0:1, 0:16], axis=AX.X)],
                   reads=["mxk"], writes=["mxk"], tag="knmax")
            ctx.op("vector", [lambda e: e.tensor_copy(sc1b[:], mxk[0:1, 16:17])],
                   reads=["mxk"], writes=["sc1b"], tag="kncast")
            ctx.op("tensor", [lambda e: e.matmul(pb[3][:, 0:1],
                       ones_sb[0:1, 0:1].broadcast_to([1, 128]), sc1b[:],
                       start=True, stop=True)],
                   reads=["sc1b", "ones"], writes=["apb3"], tag="knbc")
            ctx.op("vector", [lambda e: e.tensor_copy(maxk2b[:], pb[3][:, 0:1])],
                   reads=["apb3"], writes=["maxk2b"], tag="knbc2")

            class _QCollect:
                def __init__(s):
                    s.q = []

                def op(s, *a, **kw):
                    s.q.append((a, kw))

                def pop1(s):
                    if s.q:
                        a, kw = s.q.pop(0)
                        ctx.op(*a, **kw)

                def flush(s):
                    while s.q:
                        s.pop1()

            qctx = _QCollect()

            def emit_qproj(jt):
                par = jt % 2
                rank, off = jt // NT, (jt % NT) * 128
                QW = MCH * 128
                ctx.dma("sync", xqch[:, par * QW:(par + 1) * QW],
                        xnsf_ap[rank, :, :, off:off + 128].transpose([1, 0, 2]),
                        reads=["xnsf"], writes=[f"xq{par}", "xch"], tag=f"xq{jt}")
                ctx.dma("gpsimd", cosb[:, par * 128:(par + 1) * 128],
                        cos_in[:, jt * 128:(jt + 1) * 128],
                        writes=[f"rt{par}", "ropetbl"], tag=f"cosq{jt}")
                ctx.dma("gpsimd", sinb[:, par * 128:(par + 1) * 128],
                        sin_in[:, jt * 128:(jt + 1) * 128],
                        writes=[f"rt{par}", "ropetbl"], tag=f"sinq{jt}")
                for h in range(2):
                    fns = [(lambda e, j=j, h=h, par=par: e.matmul(
                                pb[5][:, h * 128:(h + 1) * 128],
                                waqb[h][:, j * 128:(j + 1) * 128],
                                xqch[:, par * QW + j * 128: par * QW + (j + 1) * 128],
                                start=(j == 0), stop=(j == MCH - 1)))
                           for j in range(MCH)]
                    qctx.op("tensor", fns, reads=[f"waqb{h}", f"xq{par}"],
                           writes=["apb5q"], tag=f"mmQ{jt}_{h}")
                    pe_ap = pb[5][0:64, h * 128:(h + 1) * 128]
                    po_ap = pb[5][64:128, h * 128:(h + 1) * 128]
                    csl = cosb[:, par * 128:(par + 1) * 128]
                    ssl = sinb[:, par * 128:(par + 1) * 128]
                    out_ap = QTt[h][:, par * 128:(par + 1) * 128]
                    ok = f"QTt{h}_{par}"
                    qctx.op("vector", [lambda e, a=pe_ap, c=csl: e.tensor_mul(
                               rtmp1[0:64, 0:128], a, c)],
                           reads=["apb5q", f"rt{par}"], writes=["rtmp1"],
                           tag=f"rq{jt}_{h}/ec")
                    qctx.op("vector", [lambda e, a=po_ap, s=ssl: e.tensor_mul(
                               rtmp2[0:64, 0:128], a, s)],
                           reads=["apb5q", f"rt{par}"], writes=["rtmp2"],
                           tag=f"rq{jt}_{h}/os")
                    qctx.op("vector", [lambda e, o=out_ap: e.tensor_sub(
                               o[0:64, 0:128], rtmp1[0:64, 0:128], rtmp2[0:64, 0:128])],
                           reads=["rtmp1", "rtmp2"], writes=[ok], tag=f"rq{jt}_{h}/re")
                    qctx.op("vector", [lambda e, a=pe_ap, s=ssl: e.tensor_mul(
                               rtmp1[0:64, 0:128], a, s)],
                           reads=["apb5q", f"rt{par}"], writes=["rtmp1"],
                           tag=f"rq{jt}_{h}/es")
                    qctx.op("vector", [lambda e, a=po_ap, c=csl: e.tensor_mul(
                               rtmp2[0:64, 0:128], a, c)],
                           reads=["apb5q", f"rt{par}"], writes=["rtmp2"],
                           tag=f"rq{jt}_{h}/oc")
                    qctx.op("vector", [lambda e, o=out_ap: e.tensor_add(
                               o[64:128, 0:128], rtmp1[0:64, 0:128],
                               rtmp2[0:64, 0:128])],
                           reads=["rtmp1", "rtmp2"], writes=[ok], tag=f"rq{jt}_{h}/ro")
                    qctx.op("vector", [lambda e, o=out_ap, h=h, par=par: e.tensor_mul(
                               q2b[:, h * 128:(h + 1) * 128], o, o)],
                           reads=[ok], writes=["q2b"], tag=f"q2_{jt}_{h}")
                    qctx.op("tensor", [lambda e, h=h: e.matmul(
                               pb[5][:, 256 + h:257 + h],
                               q2b[:, h * 128:(h + 1) * 128], ones_sb[:],
                               start=True, stop=True)],
                           reads=["q2b", "ones", "apb5q"], writes=["apb5q"],
                           tag=f"q2m{jt}_{h}")
                    qctx.op("scalar", [lambda e, h=h, par=par: e.activation(
                               mqs[:, (2 * h + par):(2 * h + par) + 1],
                               pb[5][:, 256 + h:257 + h], ACT.Sqrt,
                               scale=maxk2b[:])],
                           reads=["apb5q", "maxk2b"], writes=["mqs"],
                           tag=f"mq{jt}_{h}")
                    qctx.op("vector", [lambda e, h=h, par=par: e.tensor_scalar_mul(
                               nmb[:, (2 * h + par):(2 * h + par) + 1],
                               mqs[:, (2 * h + par):(2 * h + par) + 1], -1.0)],
                           reads=["mqs"], writes=[f"nmb{h}_{par}", "qm_done"],
                           tag=f"nmq{jt}_{h}")

            emit_qproj(0)
            qctx.flush()
            pending = []

            def drain(n=2):
                for _ in range(min(n, len(pending))):
                    pending.pop(0)()

            def drain_all():
                while pending:
                    pending.pop(0)()

            for jt in range(NTG):
                par = jt % 2
                n512 = (jt * 128) // 512
                nrem = jt - n512 * 4
                kext = (jt + 1) * 128
                nblk = jt + 1
                qctx.flush()
                if jt + 1 < NTG:
                    emit_qproj(jt + 1)
                drain(2)
                chunks = [(c * 512, 512, False) for c in range(n512)]
                if nrem:
                    chunks.append((n512 * 512, nrem * 128, False))
                chunks.append((jt * 128, 128, True))
                seq = []
                for ci in range(len(chunks)):
                    seq.append((0, ci))
                    seq.append((1, ci))
                sbank = [3, 4]
                obank = [2, 6]
                otk = ["apb2", "apb6"]

                def stage_a(h, ci):
                    koff, kw, is_diag = chunks[ci]
                    bk = sbank[h]
                    qsl = QTt[h][:, par * 128:(par + 1) * 128]
                    qk = f"QTt{h}_{par}"
                    nmap = nmb[:, (2 * h + par):(2 * h + par) + 1]
                    nmk = f"nmb{h}_{par}"
                    sl = (3 * h + (ci % 3)) * 512
                    ctx.op("tensor", [lambda e, koff=koff, kw=kw, qsl=qsl, bk=bk:
                               e.matmul(pb[bk][:, 0:kw], qsl,
                                        KT[:, koff:koff + kw],
                                        start=True, stop=True)],
                           reads=[qk, "KT"], writes=[f"apb{bk}"],
                           tag=f"mmS{jt}_{h}_{ci}")
                    if not is_diag:
                        ctx.op("scalar", [lambda e, kw=kw, bk=bk, sl=sl, ci=ci,
                                          h=h, nmap=nmap: e.activation(
                                   Pb[:, sl:sl + kw], pb[bk][:, 0:kw], ACT.Exp,
                                   bias=nmap,
                                   accum_out=mx[:, h * 32 + ci:h * 32 + ci + 1])],
                               reads=[f"apb{bk}", nmk, "mx"],
                               writes=[f"Pb{sl}", "mx"], tag=f"exp{jt}_{h}_{ci}")
                    else:
                        ctx.op("scalar", [lambda e, kw=kw, bk=bk, sl=sl,
                                          nmap=nmap: e.activation(
                                   Pb[:, sl:sl + kw], pb[bk][:, 0:kw], ACT.Exp,
                                   bias=nmap)],
                               reads=[f"apb{bk}", nmk], writes=[f"Pb{sl}"],
                               tag=f"expd{jt}_{h}")
                        ctx.op("vector", [lambda e, sl=sl: e.tensor_mul(
                                   Pb[:, sl:sl + 128], Pb[:, sl:sl + 128],
                                   zmask[:])],
                               reads=[f"Pb{sl}", "zmask"], writes=[f"Pb{sl}"],
                               tag=f"pmask{jt}_{h}")
                        ctx.op("vector", [lambda e, sl=sl, ci=ci, h=h:
                                   e.reduce_sum(
                                       mx[:, h * 32 + ci:h * 32 + ci + 1],
                                       Pb[:, sl:sl + 128], axis=AX.X)],
                               reads=[f"Pb{sl}"], writes=["mx"],
                               tag=f"psum{jt}_{h}")

                def stage_b1(h, ci):
                    koff, kw, is_diag = chunks[ci]
                    sl = (3 * h + (ci % 3)) * 512
                    base_blk = koff // 128
                    nb = kw // 128
                    slot = (ci % 2) * 512
                    pt = ptpsH[h]
                    ptk = f"ptps{h}_{ci % 2}"
                    pstg = PTstgH[h]
                    pstk = f"PTstg{h}_{ci % 2}"
                    tr_fns = [(lambda e, u=u, sl=sl, pt=pt, slot=slot:
                               e.matmul(
                                   pt[:, slot + u * 128:slot + (u + 1) * 128],
                                   Pb[:, sl + u * 128:sl + (u + 1) * 128],
                                   idnb[:],
                                   is_transpose=True, skip_group_check=True))
                              for u in range(nb)]
                    ctx.op("tensor", tr_fns, reads=[f"Pb{sl}", "idnb"],
                           writes=[ptk], tag=f"trP{jt}_{h}_{base_blk}")
                    ctx.op("vector", [lambda e, nb=nb, pstg=pstg, pt=pt,
                                      slot=slot:
                               e.tensor_copy(pstg[:, slot:slot + nb * 128],
                                             pt[:, slot:slot + nb * 128])],
                           reads=[ptk], writes=[pstk],
                           tag=f"evP{jt}_{h}_{base_blk}")

                def stage_b2(h, ci):
                    koff, kw, is_diag = chunks[ci]
                    base_blk = koff // 128
                    nb = kw // 128
                    slot = (ci % 2) * 512
                    pstg = PTstgH[h]
                    pstk = f"PTstg{h}_{ci % 2}"
                    pv_fns = [(lambda e, b=base_blk + u, u=u, nblk=nblk,
                                      h=h, pstg=pstg, slot=slot: e.matmul(
                                   pb[obank[h]][:, 0:128],
                                   Vb[:, b * 128:(b + 1) * 128],
                                   pstg[:, slot + u * 128:slot + (u + 1) * 128],
                                   start=(b == 0), stop=(b == nblk - 1)))
                              for u in range(nb)]
                    pv_reads = ["Vb", pstk]
                    if base_blk == 0:
                        pv_reads.append("qm_done")
                    ctx.op("tensor", pv_fns, reads=pv_reads,
                           writes=[otk[h]], tag=f"pv{jt}_{h}_{base_blk}")

                DEPTH_T = 4
                DEPTH_V = 5
                NSEQ = len(seq)
                for k in range(NSEQ + DEPTH_V):
                    if k < NSEQ:
                        stage_a(*seq[k])
                    if 0 <= k - DEPTH_T < NSEQ:
                        stage_b1(*seq[k - DEPTH_T])
                    if 0 <= k - DEPTH_V < NSEQ:
                        stage_b2(*seq[k - DEPTH_V])
                    if k >= 1:
                        drain(1)
                    qctx.pop1()
                drain_all()
                nchs = len(chunks)
                for h in range(2):
                    ctx.op("vector", [lambda e, h=h, nchs=nchs: e.reduce_sum(
                               den[:], mx[:, h * 32:h * 32 + nchs], axis=AX.X)],
                           reads=["mx"], writes=["den"], tag=f"den{jt}_{h}")
                    ctx.op("vector", [lambda e, h=h: e.reciprocal(rden[h][:],
                                                                  den[:])],
                           reads=["den"], writes=[f"rden{h}"], tag=f"rd{jt}_{h}")

                def make_tail(jt=jt, par=par):
                    ops = []
                    for h in range(2):
                        def ev_ot(h=h):
                            ctx.op("vector", [lambda e, h=h: e.tensor_copy(
                                       OTb[:, h * 128:(h + 1) * 128],
                                       pb[obank[h]][:, 0:128])],
                                   reads=[otk[h]], writes=["OTb"],
                                   tag=f"evOT{jt}_{h}")
                        ops.append(ev_ot)
                    for h in range(2):
                        for mc in range(4):
                            def rp(h=h, mc=mc):
                                ctx.op("tensor", [lambda e, mc=mc, h=h: e.matmul(
                                           pb[0][:, 0:M // 4],
                                           OTb[:, h * 128:(h + 1) * 128],
                                           waob[h][:, mc * (M // 4):
                                                    (mc + 1) * (M // 4)],
                                           start=True, stop=True)],
                                       reads=["OTb", f"waob{h}"], writes=["pbR0"],
                                       tag=f"rp{jt}_{h}_{mc}")
                            ops.append(rp)
                            if h == 0:
                                def evr(mc=mc):
                                    ctx.op("vector", [lambda e, mc=mc:
                                               e.tensor_scalar(
                                               rstgf[:, mc * (M // 4):
                                                     (mc + 1) * (M // 4)],
                                               pb[0][:, 0:M // 4],
                                               rden[0][:], None, ALU.mult)],
                                           reads=["pbR0", "rden0"], writes=["xn"],
                                           tag=f"evR0_{jt}_{mc}")
                            else:
                                def evr(mc=mc):
                                    ctx.op("vector",
                                           [lambda e, mc=mc:
                                            e.scalar_tensor_tensor(
                                               rstg[:, mc * (M // 4):
                                                    (mc + 1) * (M // 4)],
                                               pb[0][:, 0:M // 4],
                                               rden[1][:],
                                               rstgf[:, mc * (M // 4):
                                                     (mc + 1) * (M // 4)],
                                               ALU.mult, ALU.add)],
                                           reads=["pbR0", "rden1", "xn"],
                                           writes=["rstg"], tag=f"evR1_{jt}_{mc}")
                            ops.append(evr)
                    def st(jt=jt):
                        ctx.dma("gpsimd", rscr.ap()[jt * 128:(jt + 1) * 128, :],
                                rstg[:], reads=["rstg"], writes=["rscr"],
                                tag=f"stR{jt}")
                    ops.append(st)
                    return ops
                pending.extend(make_tail())
            drain_all()

            ctx.op("gpsimd",
                   [lambda e: e.collective_compute(
                       "ReduceScatter", ALU.add, replica_groups=[list(range(NC))],
                       ins=[rscr.ap().opt()], outs=[rsum.ap().opt()])],
                   reads=["rscr"], writes=["rsum"], sem="s_cc", tag="rs_r")
            for t in range(NT):
                ctx.dma("sync", rsb[:], rsum.ap()[t * 128:(t + 1) * 128, :],
                        reads=["rsum"], writes=["rstg"], tag=f"ldr{t}")
                xs_t = xs[:, t * M:(t + 1) * M]
                ctx.op("vector", [lambda e, xs_t=xs_t: e.tensor_add(xs_t, xs_t,
                                                                    rsb[:])],
                       reads=["rstg", f"xs{t}"], writes=[f"xs{t}"], tag=f"addr{t}")
        ctx.phase_barrier()

        with contextlib.ExitStack() as e3:
            sb, wk = alloc_ffn(e3, "_f2")
            emit_ffn(ctx, sb, wk, NT, M, I, NC, gsf, usf, dsf, hscr, tag="f2")
            for t in range(NT):
                ctx.dma("gpsimd", xout[t * 128:(t + 1) * 128, :],
                        xs[:, t * M:(t + 1) * M], reads=[f"xs{t}"], tag=f"stx{t}")
        for s in sorted(ctx.cnt):
            if s.startswith("qd_"):
                ctx._emit_wait("gpsimd", s, ctx.cnt[s])
        ctx.build(eg)
    global _last_ctx
    _last_ctx = ctx
    return nc


_last_ctx = None


_NC, _TPC, _M, _I, _D, _G, _H = 8, 1024, 2048, 8192, 128, 4, 4
_T = _NC * _TPC
_cache = {}


def _consts():
    if "c" not in _cache:
        F = np.float32
        half = 64
        pos = np.arange(_T, dtype=F)[:, None]
        freqs = np.power(F(10000.0), -(np.arange(half, dtype=F) / F(half)))[None, :]
        ang = (pos * freqs).astype(F)
        qkm = F(_D ** -0.25)
        cosT = np.ascontiguousarray((np.cos(ang).astype(F) * qkm).T)
        sinT = np.ascontiguousarray((np.sin(ang).astype(F) * qkm).T)
        import ml_dtypes
        idn = np.eye(128, dtype=np.float32)
        idnb = np.eye(128, dtype=ml_dtypes.bfloat16)
        i_ = np.arange(128)
        dmask = np.where(i_[:, None] < i_[None, :], F(-1e30), F(0.0)).astype(F)
        onesv = np.ones((128, 1), dtype=ml_dtypes.bfloat16)
        zmaskb = np.where(i_[:, None] < i_[None, :], 0.0, 1.0).astype(ml_dtypes.bfloat16)
        _cache["c"] = (cosT, sinT, idn, idnb, dmask, onesv, zmaskb)
    return _cache["c"]


def kernel(L0_attn_x, L0_attn_w_aq, L0_attn_w_ak, L0_attn_w_av, L0_attn_w_ao,
           L0_ffn_gate_w, L0_ffn_up_w, L0_ffn_down_w):
    cosT, sinT, idn, idnb, dmask, onesv, zmaskb = _consts()
    if "nc" not in _cache:
        _cache["nc"] = build_fused_program(_NC, _TPC, _M, _I, _D)
    nc = _cache["nc"]
    x = np.ascontiguousarray(np.asarray(L0_attn_x, dtype=np.float32).reshape(_T, _M))
    gw = np.asarray(L0_ffn_gate_w, dtype=np.float32)
    uw = np.asarray(L0_ffn_up_w, dtype=np.float32)
    dw = np.asarray(L0_ffn_down_w, dtype=np.float32)
    w_aq = np.asarray(L0_attn_w_aq, dtype=np.float32)
    w_ak = np.asarray(L0_attn_w_ak, dtype=np.float32)
    w_av = np.asarray(L0_attn_w_av, dtype=np.float32)
    w_ao = np.asarray(L0_attn_w_ao, dtype=np.float32)
    ISL = _I // _NC
    maps = []
    for c in range(_NC):
        g = c // 2
        h0, h1 = 2 * (c % 2), 2 * (c % 2) + 1
        maps.append({
            "x": np.ascontiguousarray(x[c * _TPC:(c + 1) * _TPC]),
            "gw": np.ascontiguousarray(gw[c * ISL:(c + 1) * ISL]),
            "uw": np.ascontiguousarray(uw[c * ISL:(c + 1) * ISL]),
            "dw": np.ascontiguousarray(dw[:, c * ISL:(c + 1) * ISL]),
            "wak": np.ascontiguousarray(w_ak[g]),
            "wav": np.ascontiguousarray(w_av[g]),
            "waq0": np.ascontiguousarray(w_aq[g, h0]),
            "waq1": np.ascontiguousarray(w_aq[g, h1]),
            "wao0": np.ascontiguousarray(w_ao[g, h0]),
            "wao1": np.ascontiguousarray(w_ao[g, h1]),
            "idn": idn, "idnb": idnb, "dmask": dmask,
            "onesv": onesv, "zmask": zmaskb,
            "cost": cosT, "sint": sinT,
        })
    res = run_bass_kernel_spmd(nc, maps, core_ids=list(range(_NC)))
    out = np.concatenate([res.results[i]["xout"] for i in range(_NC)], axis=0)
    return out.reshape(1, _T, _M)
